# version 14
# baseline (speedup 1.0000x reference)
import sys
if '/opt/trn_rl_repo' not in sys.path:
    sys.path.insert(0, '/opt/trn_rl_repo')
import contextlib
import numpy as np
import ml_dtypes
import concourse.bass as bass
import concourse.mybir as mybir
from concourse.alu_op_type import AluOpType as OP
from concourse.bass_utils import run_bass_kernel_spmd

F32 = mybir.dt.float32
BF16 = mybir.dt.bfloat16
AF = mybir.ActivationFunctionType
AX = mybir.AxisListType.X

NCORES = 8
L = 16384
H = 2048
INTER = 4096
DC = INTER // NCORES
NST = 16
RANK = 128
KC = 4
TS = 512
NSCH = L // TS
TOKC = L // NCORES
TB = 512
NB = TOKC // TB
NE = 16
DI = 1408
DSH = 2816
NEG = -1e30

DBG = set()


class Sem:
    def __init__(self, stack, nc, name):
        self.h = stack.enter_context(nc.semaphore(name))
        self.n = 0

    def inc(self, instr, k=1):
        instr.then_inc(self.h, k)
        self.n += k
        return (self, self.n)


def wait(eng, *evs):
    best = {}
    for e in evs:
        if e is None:
            continue
        s, n = e
        if n > best.get(id(s), (s, 0))[1]:
            best[id(s)] = (s, n)
    for s, n in best.values():
        eng.wait_ge(s.h, n)


def build():
    nc = bass.Bass("TRN2", target_bir_lowering=False, num_devices=NCORES)
    rg = [list(range(NCORES))]

    def din(name, shape, dt=F32):
        return nc.dram_tensor(name, shape, dt, kind="ExternalInput").ap()

    x_host = din("x_host", [NSCH, 128, 16 * TS])
    x_my = din("x_my", [H, TOKC])
    win_h = din("win_h", [128, 16 * 1024])
    convw = din("convw", [128, 4 * KC])
    convb = din("convb", [128, 4])
    xw_h = din("xw_h", [128, 4 * 160])
    dtwT = din("dtwT", [RANK, DC])
    dtb = din("dtb", [128, 4])
    alog = din("alog", [128, 4 * NST])
    dvec = din("dvec", [128, 4])
    op_h = din("op_h", [16, 128, 32 * 128])
    gw_h = din("gw_h", [128, NE * 16])
    up_h = din("up_h", [NE, 22, 128, 16 * 128])
    dn_h = din("dn_h", [NE, 16, 128, 11 * 128])
    sup_h = din("sup_h", [44, 128, 16 * 128])
    sdn_h = din("sdn_h", [16, 128, 22 * 128])

    out = nc.dram_tensor("out", [TOKC, H], F32, kind="ExternalOutput").ap()

    arb_i = [nc.dram_tensor(f"arb_i{p}", [160, TS], F32, kind="Internal") for p in range(2)]
    arb_o = [nc.dram_tensor(f"arb_o{p}", [160, TS], F32, kind="Internal") for p in range(2)]
    ag_i = [nc.dram_tensor(f"ag_i{p}", [DC, 2 * TS], BF16, kind="Internal") for p in range(2)]
    Y_all = nc.dram_tensor("Y_all", [NSCH // 2, INTER, 2 * TS], BF16, kind="Internal")
    scr1 = nc.dram_tensor("scr1", [H, TB], F32, kind="Internal")
    scr2 = [nc.dram_tensor(f"scr2_{j}", [H, TB], F32, kind="Internal") for j in range(2)]

    dbg = {}
    if DBG:
        if "u" in DBG:
            dbg["u"] = nc.dram_tensor("dbg_u", [DC, L], F32, kind="ExternalOutput").ap()
        if "gate" in DBG:
            dbg["gate"] = nc.dram_tensor("dbg_gate", [DC, L], F32, kind="ExternalOutput").ap()
        if "dt" in DBG:
            dbg["dt"] = nc.dram_tensor("dbg_dt", [DC, L], F32, kind="ExternalOutput").ap()
        if "ssm" in DBG:
            dbg["ssm"] = nc.dram_tensor("dbg_ssm", [160, L], F32, kind="ExternalOutput").ap()
        if "Y" in DBG:
            dbg["Y"] = nc.dram_tensor("dbg_Y", [NSCH // 2, INTER, 2 * TS], BF16, kind="ExternalOutput").ap()
        if "xt" in DBG:
            dbg["xt"] = nc.dram_tensor("dbg_xt", [NB, H, TB], F32, kind="ExternalOutput").ap()
        if "cw" in DBG:
            dbg["cw"] = nc.dram_tensor("dbg_cw", [NB, NE, TB], F32, kind="ExternalOutput").ap()
        if "g" in DBG:
            dbg["g"] = nc.dram_tensor("dbg_g", [NB, 128, 6 * NE], F32, kind="ExternalOutput").ap()
        if "acc" in DBG:
            dbg["acc"] = nc.dram_tensor("dbg_acc", [NB, H, TB], F32, kind="ExternalOutput").ap()

    ind_np = np.zeros((NST, NST * 128), dtype=np.float32)
    for n in range(NST):
        ind_np[n, 128 * n:128 * (n + 1)] = 1.0
    ind_dram = nc.inline_tensor(ind_np, name="ind_c")
    id_np = np.eye(128, dtype=np.float32)
    id_dram = nc.inline_tensor(id_np, name="id_c")

    es = contextlib.ExitStack()
    with es:
        base = (nc.sbuf_base + 31) & ~31
        cap = 207 * 1024
        es.enter_context(nc.sbuf_tensor("arena", [128, cap], mybir.dt.uint8))
        off = [base]

        def at(name, shape, dt):
            nbytes = int(np.prod(shape[1:])) * mybir.dt.size(dt)
            o = (off[0] + 31) & ~31
            t = nc.alloc_sbuf_tensor_at(name, shape, dt, offset=o)
            off[0] = o + nbytes
            return t

        s_xw = at("s_xw", [128, 4 * 160], BF16)
        s_dtw = at("s_dtw", [128, DC], BF16)
        s_A = at("s_A", [128, 4 * NST], F32)
        s_cw = at("s_cw", [128, 16], F32)
        s_cb = at("s_cb", [128, 4], F32)
        s_D = at("s_D", [128, 4], F32)
        s_dtb = at("s_dtb", [128, 4], F32)
        s_ind = at("s_ind", [NST, NST * 128], F32)
        s_on128 = at("s_on128", [128, 1], F32)
        s_sk = at("s_sk", [128, 4 * NST], F32)
        s_eps = at("s_eps", [128, 1], F32)
        s_neg = at("s_neg", [128, 1], F32)
        s_tiny = at("s_tiny", [128, 1], F32)
        ph_base = off[0]

        s_win = at("s_win", [128, 16 * 1024], BF16)
        s_x = [at(f"s_x{p}", [128, 16 * TS], BF16) for p in range(2)]
        s_upre = [at(f"s_upre{d}", [128, TS + 3], F32) for d in range(4)]
        s_u = [at(f"s_u{p}", [128, 4 * TS], F32) for p in range(2)]
        s_gate = [at(f"s_gate{p}", [128, 4 * TS], F32) for p in range(2)]
        s_dt = at("s_dt", [128, 4 * TS], F32)
        s_dtu = at("s_dtu", [128, 4 * TS], F32)
        s_ub = [at(f"s_ub{p}", [128, 4 * TS], BF16) for p in range(2)]
        s_sphi = at("s_sphi", [128, TS], F32)
        s_splo = at("s_splo", [32, TS], F32)
        s_dtr = at("s_dtr", [128, TS], F32)
        s_dtrb = at("s_dtrb", [128, TS], BF16)
        s_B = [at(f"s_B{p}", [16, TS], F32) for p in range(2)]
        s_C = [at(f"s_C{p}", [16, TS], F32) for p in range(2)]
        s_sp1 = at("s_sp1", [128, TS], F32)
        s_a = [at(f"s_a{p}", [128, 4 * TS], F32) for p in range(2)]
        s_b = at("s_b", [128, 4 * TS], F32)
        s_st = at("s_st", [128, 4 * TS], F32)
        s_yacc = at("s_yacc", [128, 4 * TS], F32)
        s_yp = at("s_yp", [128, TS], F32)
        p1_end = off[0]
        import os
        if os.environ.get('SBUFMAP'): print('SBUFMAP p1_end', p1_end, 'cap', 229376)
        assert p1_end - base < cap, f"phase1 SBUF overflow {p1_end - base}"

        off[0] = ph_base
        s_Y = at("s_Y", [128, 32 * TB], BF16)
        off[0] = ph_base
        s_acc = [at("s_acc0", [128, 16 * TB], F32)]
        s_acc.append(at("s_acc1", [128, 16 * TB], F32))
        s_xtbf = [at(f"s_xtbf{j}", [128, 16 * TB], BF16) for j in range(2)]
        s_wpA = [at(f"s_wpA{p}", [128, 32 * 128], BF16) for p in range(2)]
        s_wpB = [at(f"s_wpB{p}", [128, 22 * 128], BF16) for p in range(2)]
        s_sw = at("s_sw", [128, 22 * TB], BF16)
        s_res = [at(f"s_res{p}", [128, TB], F32) for p in range(2)]
        s_sq = [at(f"s_sq{p}", [128, TB], F32) for p in range(2)]
        s_x2 = [at(f"s_x2_{p}", [128, TB], F32) for p in range(2)]
        s_sg = [at(f"s_sg{p}", [128, TB], F32) for p in range(2)]
        s_swf = [at(f"s_swf{p}", [128, TB], F32) for p in range(2)]
        s_cwB = [at(f"s_cwB{j}", [128, TB], F32) for j in range(2)]
        s_xn = at("s_xn", [128, TB], F32)
        s_row = at("s_row", [128, H], F32)
        s_sc16 = at("s_sc16", [16, TB], F32)
        s_cwT = [at(f"s_cwT{j}", [16, TB], F32) for j in range(2)]
        s_cwtok = at("s_cwtok", [128, 4 * NE], F32)
        s_g1 = at("s_g1", [128, NE], F32)
        s_g2 = at("s_g2", [128, NE], F32)
        s_g2b = at("s_g2b", [128, NE], F32)
        s_gm = at("s_gm", [128, 4], F32)
        s_r1 = at("s_r1", [128, 1], F32)
        s_r2 = at("s_r2", [128, 1], F32)
        s_stat = at("s_stat", [1, 3 * TB], F32)
        s_mrT = at("s_mrT", [128, 8], F32)
        s_pc = at("s_pc", [128, 128], F32)
        s_gw = at("s_gw", [128, NE * 16], F32)
        s_id = at("s_id", [128, 128], F32)
        s_on1 = at("s_on1", [1, 128], F32)
        s_gdbg = at("s_gdbg", [128, 6 * NE], F32) if DBG else None
        p2_end = off[0]
        if os.environ.get('SBUFMAP'): print('SBUFMAP p2_end', p2_end, 'base', base)
        assert p2_end - base < cap, f"phase2 SBUF overflow {p2_end - base}"

        pp = [es.enter_context(nc.psum_tensor(f"pp{i}", [128, 512], F32))
              for i in range(8)]
        pready = [None] * 8
        pfree = [None] * 8

        sPE = Sem(es, nc, "sPE")
        sACT = Sem(es, nc, "sACT")
        sDVE = Sem(es, nc, "sDVE")
        sSP = Sem(es, nc, "sSP")
        sGP = Sem(es, nc, "sGP")
        sCC = Sem(es, nc, "sCC")

        block = es.enter_context(nc.Block())

        def SEC(engname):
            return {"pe": block.tensor, "act": block.scalar, "dve": block.vector,
                    "sp": block.sync, "gp": block.gpsimd}[engname]

        ev_setup_gp = [None]
        ev_setup_sp = [None]
        ev_rank = [None]
        rank_reg = [None]

        @SEC("gp")
        def _(gp):
            e = None
            e = sGP.inc(gp.dma_start(out=s_win[:], in_=win_h), 16)
            e = sGP.inc(gp.dma_start(out=s_xw[:], in_=xw_h), 16)
            e = sGP.inc(gp.dma_start(out=s_dtw[:], in_=dtwT), 16)
            ev_setup_gp[0] = e

        @SEC("sp")
        def _(sp):
            e = None
            e = sSP.inc(sp.dma_start(out=s_A[:], in_=alog), 16)
            e = sSP.inc(sp.dma_start(out=s_cw[:], in_=convw), 16)
            e = sSP.inc(sp.dma_start(out=s_cb[:], in_=convb), 16)
            e = sSP.inc(sp.dma_start(out=s_D[:], in_=dvec), 16)
            e = sSP.inc(sp.dma_start(out=s_dtb[:], in_=dtb), 16)
            e = sSP.inc(sp.dma_start(out=s_ind[:], in_=ind_dram.ap()), 16)
            ev_setup_sp[0] = e
            rk = sp.alloc_register()
            sp.cc_rank_ld(rk, replica_groups=rg)
            rank_reg[0] = rk
            ev_rank[0] = (sSP, sSP.n)

        ev_A = [None]
        ev_init = [None]

        @SEC("act")
        def _(sc):
            wait(sc, ev_setup_sp[0])
            sc.activation(s_A[:], s_A[:], AF.Exp)
            ev_A[0] = sACT.inc(sc.mul(s_A[:], s_A[:], -1.0))

        @SEC("dve")
        def _(v):
            v.memset(s_sk[:], 0.0)
            for d in range(4):
                v.memset(s_upre[d][:, 0:3], 0.0)
            v.memset(s_on128[:], 1.0)
            v.memset(s_eps[:], 1e-5)
            v.memset(s_neg[:], NEG)
            v.memset(s_tiny[:], 1e-20)
            ev_init[0] = sDVE.inc(v.memset(s_sk[:, 0:1], 0.0))

        ev_x = [None, None]
        ev_x_free = [None, None]
        ev_upre_free = [None]
        ev_ug = [None, None]
        ev_ug_free = [None, None]
        ev_sphl = [None]
        ev_ar_out = [None, None]
        ev_ar_in_free = [None, None]
        ev_dtrb = [None]
        ev_dt = [None]
        ev_a = [None, None]
        ev_a_free = [None, None]
        ev_bc = [None, None]
        ev_bc_free = [None, None]
        ev_y = [None]
        ev_ag_in = [None, None]
        ev_ag_done = [None] * NSCH
        ev_sk_dma = [None]

        def stageA(s):
            par = s % 2

            @SEC("gp")
            def _(gp, s=s, par=par):
                wait(gp, ev_x_free[par], ev_setup_gp[0])
                ev_x[par] = sGP.inc(gp.dma_start(out=s_x[par][:], in_=x_host[s]), 16)

            for m in range(8):
                bank = m % 2

                @SEC("pe")
                def _(te, s=s, par=par, m=m, bank=bank):
                    wait(te, pfree[bank], ev_x[par], ev_A[0], ev_init[0])
                    for k in range(16):
                        ins = te.matmul(pp[bank][:], s_win[:, 1024 * k + 128 * m:1024 * k + 128 * (m + 1)],
                                        s_x[par][:, TS * k:TS * (k + 1)],
                                        start=(k == 0), stop=(k == 15))
                    e = sPE.inc(ins)
                    pready[bank] = e
                    if m == 7:
                        ev_x_free[par] = e

                @SEC("act")
                def _(sc, s=s, par=par, m=m, bank=bank):
                    wait(sc, pready[bank])
                    if m < 4:
                        wait(sc, ev_upre_free[0])
                        ins = sc.copy(s_upre[m][:, 3:3 + TS], pp[bank][:])
                    else:
                        wait(sc, ev_ug_free[par])
                        ins = sc.activation(s_gate[par][:, TS * (m - 4):TS * (m - 3)],
                                            pp[bank][:], AF.Silu)
                    pfree[bank] = sACT.inc(ins)

            @SEC("dve")
            def _(v, s=s, par=par):
                wait(v, pfree[0], pfree[1])
                for d in range(4):
                    v.tensor_scalar(s_u[par][:, TS * d:TS * (d + 1)], s_upre[d][:, 0:TS],
                                    s_cw[:, 4 * d:4 * d + 1], s_cb[:, d:d + 1],
                                    op0=OP.mult, op1=OP.add)
                    for j in range(1, 4):
                        v.scalar_tensor_tensor(s_u[par][:, TS * d:TS * (d + 1)],
                                               s_upre[d][:, j:j + TS],
                                               s_cw[:, 4 * d + j:4 * d + j + 1],
                                               s_u[par][:, TS * d:TS * (d + 1)],
                                               op0=OP.mult, op1=OP.add)
                ins = None
                for d in range(4):
                    ins = v.tensor_copy(s_upre[d][:, 0:3], s_upre[d][:, TS:TS + 3])
                ev_upre_free[0] = sDVE.inc(ins)

            @SEC("act")
            def _(sc, s=s, par=par):
                wait(sc, ev_upre_free[0], ev_ub_free[par])
                sc.activation(s_u[par][:], s_u[par][:], AF.Silu)
                ev_ug[par] = sACT.inc(sc.copy(s_ub[par][:], s_u[par][:]))

            @SEC("pe")
            def _(te, s=s, par=par):
                wait(te, ev_ug[par], pfree[2], pfree[3])
                for k in range(4):
                    i1 = te.matmul(pp[2][:], s_xw[:, 160 * k:160 * k + 128],
                                   s_ub[par][:, TS * k:TS * (k + 1)],
                                   start=(k == 0), stop=(k == 3))
                for k in range(4):
                    i2 = te.matmul(pp[3][0:32, :], s_xw[:, 160 * k + 128:160 * (k + 1)],
                                   s_ub[par][:, TS * k:TS * (k + 1)],
                                   start=(k == 0), stop=(k == 3))
                pready[2] = pready[3] = sPE.inc(i2)
                ev_ub_free[par] = pready[2]

            @SEC("act")
            def _(sc, s=s, par=par):
                wait(sc, pready[2], stA_ar.get(s - 1))
                sc.copy(s_sphi[:], pp[2][:])
                ins = sc.copy(s_splo[:], pp[3][0:32, :])
                pfree[2] = pfree[3] = ev_sphl[0] = sACT.inc(ins)

            @SEC("gp")
            def _(gp, s=s, par=par):
                wait(gp, ev_sphl[0], ev_ar_in_free[par])
                sGP.inc(gp.dma_start(out=arb_i[par].ap()[0:128, :], in_=s_sphi[:]), 16)
                ev = sGP.inc(gp.dma_start(out=arb_i[par].ap()[128:160, :], in_=s_splo[:]), 16)
                stA_ar[s] = ev

            @SEC("gp")
            def _(gp, s=s, par=par):
                wait(gp, stA_ar[s], ev_ar_out[par])
                ins = gp.collective_compute(
                    "AllReduce", OP.add, replica_groups=rg,
                    ins=[arb_i[par].ap()], outs=[arb_o[par].ap()])
                e = sCC.inc(ins)
                ev_ar_in_free[par] = e
                stB_ar[s] = e

        stA_ar = {}
        stB_ar = {}

        def stageB(s):
            par = s % 2

            @SEC("sp")
            def _(sp, s=s, par=par):
                wait(sp, stB_ar[s], ev_dtrb[0], stB_bclast.get(s - 2))
                sSP.inc(sp.dma_start(out=s_dtr[:], in_=arb_o[par].ap()[0:128, :]), 16)
                sSP.inc(sp.dma_start(out=s_B[par][:], in_=arb_o[par].ap()[128:144, :]), 16)
                ev = sSP.inc(sp.dma_start(out=s_C[par][:], in_=arb_o[par].ap()[144:160, :]), 16)
                ev_ar_out[par] = ev
                stB_ld[s] = ev
                if "ssm" in dbg:
                    sSP.inc(sp.dma_start(out=dbg["ssm"][:, s * TS:(s + 1) * TS],
                                         in_=arb_o[par].ap()), 16)

            @SEC("act")
            def _(sc, s=s):
                wait(sc, stB_ld[s], stB_dtpe.get(s - 1))
                ev_dtrb[0] = sACT.inc(sc.copy(s_dtrb[:], s_dtr[:]))

            for m in range(4):
                bank = 4 + (m % 2)

                @SEC("pe")
                def _(te, s=s, m=m, bank=bank):
                    wait(te, pfree[bank], ev_dtrb[0])
                    ins = te.matmul(pp[bank][:], s_dtw[:, 128 * m:128 * (m + 1)],
                                    s_dtrb[:], start=True, stop=True)
                    pready[bank] = sPE.inc(ins)
                    if m == 3:
                        stB_dtpe[s] = pready[bank]

                @SEC("act")
                def _(sc, s=s, m=m, bank=bank):
                    wait(sc, pready[bank], ev_dt[0], stB_alast.get(s - 1))
                    sc.activation(s_sp1[:], pp[bank][:], AF.Exp, bias=s_dtb[:, m:m + 1])
                    ins = sc.activation(s_dt[:, TS * m:TS * (m + 1)], s_sp1[:], AF.Ln,
                                        bias=1.0)
                    e = sACT.inc(ins)
                    pfree[bank] = e
                    stB_dt[(s, m)] = e

            @SEC("dve")
            def _(v, s=s, par=par):
                wait(v, ev_ug[par], *[stB_dt[(s, m)] for m in range(4)])
                ins = None
                for d in range(4):
                    ins = v.tensor_tensor(s_dtu[:, TS * d:TS * (d + 1)],
                                          s_dt[:, TS * d:TS * (d + 1)],
                                          s_u[par][:, TS * d:TS * (d + 1)], op=OP.mult)
                ev_dt[0] = sDVE.inc(ins)
                if "dt" in dbg or "u" in dbg or "gate" in dbg:
                    stB_dbg[s] = ev_dt[0]

            if dbg:
                @SEC("sp")
                def _(sp, s=s, par=par):
                    if "dt" in dbg or "u" in dbg or "gate" in dbg:
                        wait(sp, stB_dbg.get(s))
                        for d in range(4):
                            sl = slice(s * TS, (s + 1) * TS)
                            if "u" in dbg:
                                sSP.inc(sp.dma_start(out=dbg["u"][128 * d:128 * (d + 1), sl],
                                                     in_=s_u[par][:, TS * d:TS * (d + 1)]), 16)
                            if "gate" in dbg:
                                sSP.inc(sp.dma_start(out=dbg["gate"][128 * d:128 * (d + 1), sl],
                                                     in_=s_gate[par][:, TS * d:TS * (d + 1)]), 16)
                            if "dt" in dbg:
                                sSP.inc(sp.dma_start(out=dbg["dt"][128 * d:128 * (d + 1), sl],
                                                     in_=s_dt[:, TS * d:TS * (d + 1)]), 16)

            for n in range(NST):
                npar = n % 2
                bbank = 6 if npar == 0 else 4
                cbank = 7 if npar == 0 else 5

                @SEC("pe")
                def _(te, s=s, n=n, npar=npar):
                    wait(te, stB_ld[s], pfree[bbank], pfree[cbank])
                    te.matmul(pp[bbank][:], s_ind[:, 128 * n:128 * (n + 1)], s_B[par][:],
                              start=True, stop=True)
                    ins = te.matmul(pp[cbank][:], s_ind[:, 128 * n:128 * (n + 1)], s_C[par][:],
                                    start=True, stop=True)
                    pready[bbank] = pready[cbank] = sPE.inc(ins)
                    if n == NST - 1:
                        stB_bclast[s] = pready[bbank]

                @SEC("act")
                def _(sc, s=s, n=n, npar=npar):
                    wait(sc, ev_a_free[npar], ev_dt[0])
                    ins = None
                    for d in range(4):
                        ins = sc.activation(s_a[npar][:, TS * d:TS * (d + 1)],
                                            s_dt[:, TS * d:TS * (d + 1)], AF.Exp,
                                            scale=s_A[:, NST * d + n:NST * d + n + 1])
                    e = sACT.inc(ins)
                    ev_a[npar] = e
                    if n == NST - 1:
                        stB_alast[s] = e

                @SEC("dve")
                def _(v, s=s, n=n, npar=npar, bbank=bbank, cbank=cbank):
                    wait(v, ev_a[npar], pready[bbank])
                    if n == 0:
                        wait(v, ev_ag_sbuf_free[0])
                    bc4 = pp[bbank][:].unsqueeze(1).broadcast_to([128, 4, TS])
                    v.tensor_tensor(s_b[:].rearrange("p (d t) -> p d t", d=4),
                                    s_dtu[:].rearrange("p (d t) -> p d t", d=4),
                                    bc4, op=OP.mult)
                    for d in range(4):
                        dsl = slice(TS * d, TS * (d + 1))
                        v.tensor_tensor_scan(s_st[:, dsl], s_a[npar][:, dsl], s_b[:, dsl],
                                             s_sk[:, NST * d + n:NST * d + n + 1],
                                             op0=OP.mult, op1=OP.add)
                    cc4 = pp[cbank][:].unsqueeze(1).broadcast_to([128, 4, TS])
                    if n == 0:
                        ins = v.tensor_tensor(s_yacc[:].rearrange("p (d t) -> p d t", d=4),
                                              s_st[:].rearrange("p (d t) -> p d t", d=4),
                                              cc4, op=OP.mult)
                    else:
                        v.tensor_tensor(s_b[:].rearrange("p (d t) -> p d t", d=4),
                                        s_st[:].rearrange("p (d t) -> p d t", d=4),
                                        cc4, op=OP.mult)
                        ins = v.tensor_tensor(s_yacc[:], s_yacc[:], s_b[:], op=OP.add)
                    ins = v.tensor_copy(
                        s_sk[:].rearrange("p (d n2) -> p d n2", d=4)[:, :, n:n + 1],
                        s_st[:].rearrange("p (d t) -> p d t", d=4)[:, :, TS - 1:TS])
                    e = sDVE.inc(ins)
                    pfree[bbank] = pfree[cbank] = e
                    ev_a_free[npar] = e

            @SEC("dve")
            def _(v, s=s, par=par):
                wait(v, ev_ag_sbuf_free[0])
                ins = None
                for d in range(4):
                    dsl = slice(TS * d, TS * (d + 1))
                    v.scalar_tensor_tensor(s_yp[:], s_u[par][:, dsl], s_D[:, d:d + 1],
                                           s_yacc[:, dsl], op0=OP.mult, op1=OP.add)
                    ins = v.tensor_tensor(s_yacc[:, dsl], s_yp[:], s_gate[par][:, dsl],
                                          op=OP.mult)
                e = sDVE.inc(ins)
                ev_ug_free[par] = e
                stB_y[s] = e

            @SEC("gp")
            def _(gp, s=s, par=par):
                q = s // 2
                half = s % 2
                wait(gp, stB_y[s])
                if half == 0:
                    wait(gp, ev_ag_i_free[q % 2])
                e = None
                for d in range(4):
                    e = sGP.inc(gp.dma_start(
                        out=ag_i[q % 2].ap()[128 * d:128 * (d + 1), TS * half:TS * (half + 1)],
                        in_=s_yacc[:, TS * d:TS * (d + 1)]), 16)
                ev_ag_in[half] = e
                ev_ag_sbuf_free[0] = e

            if s % 2 == 1:
                @SEC("gp")
                def _(gp, s=s):
                    q = s // 2
                    wait(gp, ev_ag_in[0], ev_ag_in[1])
                    ins = gp.collective_compute(
                        "AllGather", OP.bypass, replica_groups=rg,
                        ins=[ag_i[q % 2].ap()], outs=[Y_all.ap()[q]])
                    e = sCC.inc(ins)
                    ev_ag_done[q] = e
                    ev_ag_i_free[q % 2] = e

        stB_ld = {}
        stB_dt = {}
        stB_alast = {}
        stB_bclast = {}
        stB_dtpe = {}
        stB_dbg = {}
        stB_y = {}
        ev_ag_sbuf_free = [None]
        ev_ag_i_free = [None, None]
        ev_ub_free = [None, None]

        for s in range(NSCH + 1):
            if s < NSCH:
                stageA(s)
            if s >= 1:
                stageB(s - 1)

        if "Y" in dbg:
            @SEC("gp")
            def _(gp):
                wait(gp, *[ev_ag_done[q] for q in range(NSCH // 2)])
                sGP.inc(gp.dma_start(out=dbg["Y"], in_=Y_all.ap()), 16)

        CHUNK_ELEMS = INTER * TS

        ev_Y = [None]
        ev_Y_free = [None]
        ev_acc_free = [None, None]
        ev_xtbf_free = [None, None]
        ev_wpA = [None, None]
        ev_wpA_free = [None, None]
        ev_wpB = [None, None]
        ev_wpB_free = [None, None]
        ev_sw_free = [None]

        def p2_partA(b):
            j = b % 2

            @SEC("sp")
            def _(sp, b=b):
                wait(sp, ev_rank[0], ev_acc_free[0], ev_ag_done[NSCH // 2 - 1], ev_Y_free[0])
                rk = rank_reg[0]
                with sp.register(f"yoff{b}") as yo, sp.register(f"yh{b}") as yh:
                    sp.reg_mul(yo, rk, 4)
                    sp.reg_add(yo, yo, b)
                    sp.reg_alu(yh, yo, 1, OP.bitwise_and)
                    sp.reg_mul(yh, yh, TS)
                    sp.reg_alu(yo, yo, 1, OP.arith_shift_right)
                    sp.reg_mul(yo, yo, INTER * 2 * TS)
                    sp.reg_alu(yo, yo, yh, OP.add)
                    ysrc = bass.AP(Y_all, yo, [[2 * TS, INTER], [1, TS]])
                    ysrc = ysrc.rearrange("(k p) t -> p k t", p=128)
                    ev_Y[0] = sSP.inc(sp.dma_start(out=s_Y[:].rearrange("p (k t) -> p k t", k=32), in_=ysrc), 16)

            for m in range(16):
                bank = m % 2

                @SEC("gp")
                def _(gp, b=b, m=m):
                    par = m % 2
                    wait(gp, ev_wpA_free[par])
                    if b == 0 and m == 0:
                        wait(gp, ev_ag_done[NSCH // 2 - 1])
                    ev_wpA[par] = sGP.inc(gp.dma_start(out=s_wpA[par][:], in_=op_h[m]), 16)

                @SEC("pe")
                def _(te, b=b, m=m, bank=bank):
                    par = m % 2
                    wait(te, pfree[bank], ev_Y[0], ev_wpA[par])
                    if b == 0 and m == 0:
                        wait(te, ev_p2c[0], ev_p2c2[0])
                    for k in range(32):
                        ins = te.matmul(pp[bank][:], s_wpA[par][:, 128 * k:128 * (k + 1)],
                                        s_Y[:, TB * k:TB * (k + 1)],
                                        start=(k == 0), stop=(k == 31))
                    e = sPE.inc(ins)
                    pready[bank] = e
                    ev_wpA_free[par] = e
                    if m == 15:
                        ev_Y_free[0] = e

                @SEC("sp")
                def _(sp, b=b, m=m):
                    wait(sp, p2_xuse.get((b, m - 2)), p2_x2use.get((b - 1, 14 + (m % 2))))
                    p2_xld[(b, m)] = sSP.inc(
                        sp.dma_start(out=s_x2[m % 2][:], in_=x_my[128 * m:128 * (m + 1),
                                                                  TB * b:TB * (b + 1)]), 16)

                @SEC("dve")
                def _(v, b=b, m=m, bank=bank):
                    wait(v, pready[bank], p2_xld[(b, m)], p2_resfree.get((b, m - 2)),
                         p2_xnuse.get((b - 1, 14 + (m % 2))))
                    ins = v.tensor_tensor(s_res[m % 2][:], pp[bank][:], s_x2[m % 2][:], op=OP.add)
                    e = sDVE.inc(ins)
                    pfree[bank] = e
                    p2_res[(b, m)] = e
                    p2_xuse[(b, m)] = e

                @SEC("act")
                def _(sc, b=b, m=m):
                    wait(sc, p2_res[(b, m)], p2_stats.get((b, m - 1)))
                    p2_sq[(b, m)] = sACT.inc(sc.square(s_sq[m % 2][:], s_res[m % 2][:]))

                @SEC("pe")
                def _(te, b=b, m=m):
                    wait(te, p2_res[(b, m)], p2_sq[(b, m)])
                    if m == 0:
                        wait(te, pfree[4], pfree[5])
                    te.matmul(pp[4][0:1, :], s_on128[:], s_res[m % 2][:],
                              start=(m == 0), stop=(m == 15))
                    ins = te.matmul(pp[5][0:1, :], s_on128[:], s_sq[m % 2][:],
                                    start=(m == 0), stop=(m == 15))
                    e = sPE.inc(ins)
                    p2_stats[(b, m)] = e
                    if m == 15:
                        pready[4] = pready[5] = e

                @SEC("sp")
                def _(sp, b=b, m=m):
                    wait(sp, p2_res[(b, m)], p2_stats[(b, m)])
                    ins = sp.dma_start(out=scr1.ap()[128 * m:128 * (m + 1), :], in_=s_res[m % 2][:])
                    e = sSP.inc(ins, 16)
                    p2_resfree[(b, m)] = e

            @SEC("act")
            def _(sc, b=b):
                wait(sc, pready[4])
                sc.mul(s_stat[:, 0 * TB:1 * TB], pp[4][0:1, :], 1.0 / H)
                ins = sc.mul(s_stat[:, 1 * TB:2 * TB], pp[5][0:1, :], 1.0 / H)
                e = sACT.inc(ins)
                pfree[4] = pfree[5] = e
                p2_st1[b] = e

            @SEC("dve")
            def _(v, b=b):
                wait(v, p2_st1[b])
                v.tensor_tensor(s_stat[:, 2 * TB:3 * TB], s_stat[:, 0:TB],
                                s_stat[:, 0:TB], op=OP.mult)
                ins = v.tensor_tensor(s_stat[:, TB:2 * TB], s_stat[:, TB:2 * TB],
                                      s_stat[:, 2 * TB:3 * TB], op=OP.subtract)
                p2_st2[b] = sDVE.inc(ins)

            @SEC("act")
            def _(sc, b=b):
                wait(sc, p2_st2[b])
                sc.activation(s_stat[:, TB:2 * TB], s_stat[:, TB:2 * TB],
                              AF.Ln, bias=s_eps[0:1, :])
                ins = sc.activation(s_stat[:, TB:2 * TB], s_stat[:, TB:2 * TB],
                                    AF.Exp, scale=-0.5)
                p2_st3[b] = sACT.inc(ins)

            @SEC("dve")
            def _(v, b=b):
                wait(v, p2_st3[b])
                ins = v.tensor_tensor(s_stat[:, 2 * TB:3 * TB], s_stat[:, 0:TB],
                                      s_stat[:, TB:2 * TB], op=OP.mult)
                sDVE.inc(ins)
                v.wait_ge(sDVE.h, sDVE.n)
                ins = v.tensor_scalar(s_stat[:, 2 * TB:3 * TB], s_stat[:, 2 * TB:3 * TB],
                                      -1.0, 1.0, op0=OP.mult, op1=OP.mult)
                p2_st4[b] = sDVE.inc(ins)

            @SEC("pe")
            def _(te, b=b):
                wait(te, p2_st4[b], pfree[6], pfree[7])
                te.matmul(pp[6][:], s_on1[:], s_stat[0:1, TB:2 * TB],
                          start=True, stop=True)
                ins = te.matmul(pp[7][:], s_on1[:], s_stat[0:1, 2 * TB:3 * TB],
                                start=True, stop=True)
                e = sPE.inc(ins)
                pready[6] = pready[7] = e

            for m in range(16):
                @SEC("sp")
                def _(sp, b=b, m=m):
                    wait(sp, p2_xnuse.get((b, m - 2)), p2_resfree.get((b, 14 + (m % 2))))
                    p2_r2ld[(b, m)] = sSP.inc(
                        sp.dma_start(out=s_res[m % 2][:], in_=scr1.ap()[128 * m:128 * (m + 1), :]), 16)

                @SEC("dve")
                def _(v, b=b, m=m):
                    wait(v, p2_r2ld[(b, m)], pready[6], p2_xn_free2[0])
                    v.tensor_tensor(s_xn[:], s_res[m % 2][:], pp[6][:], op=OP.mult)
                    ins = v.tensor_tensor(s_xn[:], s_xn[:], pp[7][:], op=OP.add)
                    p2_xn[(b, m)] = sDVE.inc(ins)
                    p2_xnuse[(b, m)] = ins and p2_xn[(b, m)]

                @SEC("pe")
                def _(te, b=b, m=m):
                    wait(te, p2_xn[(b, m)])
                    if m == 0:
                        wait(te, pfree[5])
                    ins = te.matmul(pp[5][0:16, :], s_gw[:, 16 * m:16 * (m + 1)], s_xn[:],
                                    start=(m == 0), stop=(m == 15))
                    e = sPE.inc(ins)
                    p2_gsc[(b, m)] = e
                    if m == 15:
                        pready[5] = e

                @SEC("act")
                def _(sc, b=b, m=m, j=j):
                    wait(sc, p2_xn[(b, m)], ev_xtbf_free[j])
                    ins = sc.copy(s_xtbf[j][:, TB * m:TB * (m + 1)], s_xn[:])
                    p2_xtbf[(b, m)] = sACT.inc(ins)

                @SEC("sp")
                def _(sp, b=b, m=m, j=j):
                    wait(sp, p2_xn[(b, m)], p2_gsc[(b, m)], p2_xtbf[(b, m)])
                    ins = sp.dma_start(out=scr2[j].ap()[128 * m:128 * (m + 1), :], in_=s_xn[:])
                    e = sSP.inc(ins, 16)
                    p2_xn_free[0] = e
                    p2_xn_free2[0] = e
                    if m == 15:
                        pfree[6] = pfree[7] = e

            if "xt" in dbg:
                @SEC("sp")
                def _(sp, b=b, j=j):
                    wait(sp, p2_xn_free[0])
                    sSP.inc(sp.dma_start(out=dbg["xt"][b], in_=scr2[j].ap()), 16)

            @SEC("act")
            def _(sc, b=b):
                wait(sc, pready[5])
                ins = sc.copy(s_sc16[:], pp[5][0:16, :])
                e = sACT.inc(ins)
                pfree[5] = e
                p2_sc16[b] = e

            for t in range(4):
                bank = 6 + (t % 2)

                @SEC("pe")
                def _(te, b=b, t=t, bank=bank):
                    wait(te, p2_sc16[b], pfree[bank])
                    ins = te.transpose(pp[bank][0:128, 0:16], s_sc16[:, 128 * t:128 * (t + 1)],
                                       s_id[0:16, 0:16])
                    pready[bank] = sPE.inc(ins)

                @SEC("act")
                def _(sc, b=b, t=t, bank=bank):
                    wait(sc, pready[bank])
                    ins = sc.copy(s_cwtok[:, NE * t:NE * (t + 1)], pp[bank][0:128, 0:16])
                    pfree[bank] = sACT.inc(ins)
                    p2_sctok[(b, t)] = (sACT, sACT.n)

            for t in range(4):
                @SEC("dve")
                def _(v, b=b, t=t):
                    wait(v, p2_sctok[(b, t)])
                    cwt = s_cwtok[:, NE * t:NE * (t + 1)]

                    def F(ins):
                        sDVE.inc(ins)
                        v.wait_ge(sDVE.h, sDVE.n)
                    F(v.tensor_reduce(s_r1[:], cwt, axis=AX, op=OP.max))
                    F(v.tensor_scalar(s_r1[:], s_r1[:], -1.0, 1.0, op0=OP.mult, op1=OP.mult))
                    p2_sm[(b, t)] = sDVE.inc(v.tensor_copy(s_g1[:], cwt))

                @SEC("act")
                def _(sc, b=b, t=t):
                    wait(sc, p2_sm[(b, t)])
                    cwt = s_cwtok[:, NE * t:NE * (t + 1)]
                    ins = sc.activation(cwt, s_g1[:], AF.Exp, bias=s_r1[:, 0:1])
                    p2_smx[(b, t)] = sACT.inc(ins)

                @SEC("dve")
                def _(v, b=b, t=t):
                    wait(v, p2_smx[(b, t)])
                    cwt = s_cwtok[:, NE * t:NE * (t + 1)]

                    def F(ins):
                        sDVE.inc(ins)
                        v.wait_ge(sDVE.h, sDVE.n)
                    F(v.tensor_reduce(s_r1[:], cwt, axis=AX, op=OP.add))
                    F(v.reciprocal(s_r2[:], s_r1[:]))
                    F(v.tensor_scalar(cwt, cwt, s_r2[:, 0:1], 1.0, op0=OP.mult, op1=OP.mult))
                    gv = cwt.rearrange("p (g e) -> p g e", g=4)
                    F(v.tensor_reduce(s_gm[:], gv, axis=AX, op=OP.max))
                    F(v.tensor_reduce(s_r1[:], s_gm[:], axis=AX, op=OP.max))
                    F(v.tensor_scalar(s_g2[:, 0:4], s_gm[:], s_r1[:, 0:1], 1.0,
                                      op0=OP.is_ge, op1=OP.mult))
                    F(v.scalar_tensor_tensor(s_g2[:, 0:4], s_g2[:, 0:4], s_neg[:, 0:1], s_gm[:],
                                             op0=OP.mult, op1=OP.add))
                    F(v.tensor_reduce(s_r1[:], s_g2[:, 0:4], axis=AX, op=OP.max))
                    F(v.tensor_scalar(s_gm[:], s_gm[:], s_r1[:, 0:1], 1.0,
                                      op0=OP.is_ge, op1=OP.mult))
                    gmb = s_gm[:].unsqueeze(2).broadcast_to([128, 4, 4])
                    F(v.tensor_tensor(s_g1[:].rearrange("p (g e) -> p g e", g=4), gv, gmb,
                                      op=OP.mult))
                    F(v.tensor_copy(s_g2[:], s_g1[:]))
                    for _i in range(3):
                        F(v.tensor_reduce(s_r1[:], s_g2[:], axis=AX, op=OP.max))
                        F(v.tensor_scalar(s_g2b[:], s_g2[:], s_r1[:, 0:1], 1.0,
                                          op0=OP.is_ge, op1=OP.mult))
                        F(v.scalar_tensor_tensor(s_g2[:], s_g2b[:], s_neg[:, 0:1], s_g2[:],
                                                 op0=OP.mult, op1=OP.add))
                    F(v.tensor_reduce(s_r1[:], s_g2[:], axis=AX, op=OP.max))
                    F(v.tensor_scalar(s_g2b[:], s_g1[:], s_r1[:, 0:1], 1.0,
                                      op0=OP.is_ge, op1=OP.mult))
                    F(v.tensor_tensor(s_g1[:], s_g1[:], s_g2b[:], op=OP.mult))
                    F(v.tensor_reduce(s_r1[:], s_g1[:], axis=AX, op=OP.add))
                    F(v.tensor_scalar(s_r1[:], s_r1[:], s_tiny[:, 0:1], 1.0, op0=OP.add, op1=OP.mult))
                    F(v.reciprocal(s_r2[:], s_r1[:]))
                    ins = v.tensor_scalar(cwt, s_g1[:], s_r2[:, 0:1], 1.0,
                                          op0=OP.mult, op1=OP.mult)
                    p2_cw[(b, t)] = sDVE.inc(ins)

            for t in range(4):
                bank = 6 + (t % 2)

                @SEC("pe")
                def _(te, b=b, t=t, bank=bank):
                    wait(te, p2_cw[(b, t)], pfree[bank])
                    ins = te.transpose(pp[bank][0:16, 0:128],
                                       s_cwtok[:, NE * t:NE * (t + 1)], s_id[:])
                    pready[bank] = sPE.inc(ins)

                @SEC("act")
                def _(sc, b=b, t=t, bank=bank, j=j):
                    wait(sc, pready[bank])
                    ins = sc.copy(s_cwT[j][:, 128 * t:128 * (t + 1)], pp[bank][0:16, 0:128])
                    e = sACT.inc(ins)
                    pfree[bank] = e
                    p2_cwT[(b, t)] = e

            if "cw" in dbg:
                @SEC("sp")
                def _(sp, b=b, j=j):
                    wait(sp, *[p2_cwT[(b, t)] for t in range(4)])
                    sSP.inc(sp.dma_start(out=dbg["cw"][b], in_=s_cwT[j][:]), 16)

        def p2_experts_pair(bp):
            b0, b1 = 2 * bp, 2 * bp + 1
            for e in range(NE):
                @SEC("pe")
                def _(te, bp=bp, e=e):
                    wait(te, *[p2_cwT[(b0, t)] for t in range(4)],
                         *[p2_cwT[(b1, t)] for t in range(4)], pfree[6], pfree[7])
                    te.matmul(pp[6][:], s_ind[:, 128 * e:128 * (e + 1)], s_cwT[0][:],
                              start=True, stop=True)
                    ins = te.matmul(pp[7][:], s_ind[:, 128 * e:128 * (e + 1)], s_cwT[1][:],
                                    start=True, stop=True)
                    pready[6] = pready[7] = sPE.inc(ins)

                @SEC("act")
                def _(sc, bp=bp, e=e):
                    wait(sc, pready[6], p2_cwB_free[0], p2_cwB_free[1])
                    sc.copy(s_cwB[0][:], pp[6][:])
                    ins = sc.copy(s_cwB[1][:], pp[7][:])
                    e2 = sACT.inc(ins)
                    pfree[6] = pfree[7] = e2
                    p2_cwB[0] = p2_cwB[1] = e2

                for mp in range(11):
                    B0 = (mp % 2) * 4

                    @SEC("gp")
                    def _(gp, bp=bp, e=e, mp=mp):
                        parA = mp % 2
                        wait(gp, ev_wpA_free[parA])
                        sGP.inc(gp.dma_start(out=s_wpA[parA][:, 0:16 * 128], in_=up_h[e][mp]), 16)
                        ins = gp.dma_start(out=s_wpA[parA][:, 16 * 128:32 * 128], in_=up_h[e][mp + 11])
                        ev_wpA[parA] = sGP.inc(ins, 16)

                    @SEC("pe")
                    def _(te, bp=bp, e=e, mp=mp, B0=B0):
                        parA = mp % 2
                        wait(te, ev_wpA[parA], pfree[B0], pfree[B0 + 1], pfree[B0 + 2],
                             pfree[B0 + 3])
                        if mp == 0 and e == 0:
                            wait(te, *[p2_xtbf[(b0, m)] for m in range(16)],
                                 *[p2_xtbf[(b1, m)] for m in range(16)])
                        for k in range(16):
                            wslice = s_wpA[parA][:, 128 * k:128 * (k + 1)]
                            te.matmul(pp[B0][:], wslice, s_xtbf[0][:, TB * k:TB * (k + 1)],
                                      start=(k == 0), stop=(k == 15))
                            te.matmul(pp[B0 + 2][:], wslice, s_xtbf[1][:, TB * k:TB * (k + 1)],
                                      start=(k == 0), stop=(k == 15))
                        for k in range(16):
                            wslice = s_wpA[parA][:, 16 * 128 + 128 * k:16 * 128 + 128 * (k + 1)]
                            te.matmul(pp[B0 + 1][:], wslice, s_xtbf[0][:, TB * k:TB * (k + 1)],
                                      start=(k == 0), stop=(k == 15))
                            ins = te.matmul(pp[B0 + 3][:], wslice, s_xtbf[1][:, TB * k:TB * (k + 1)],
                                            start=(k == 0), stop=(k == 15))
                        ee = sPE.inc(ins)
                        for i in range(4):
                            pready[B0 + i] = ee
                        ev_wpA_free[parA] = ee

                    for jj in range(2):
                        @SEC("act")
                        def _(sc, bp=bp, e=e, mp=mp, jj=jj, B0=B0):
                            wait(sc, pready[B0 + 2 * jj], p2_sg_free[jj])
                            ins = sc.activation(s_sg[jj][:], pp[B0 + 2 * jj][:], AF.Silu)
                            p2_sg[jj] = sACT.inc(ins)

                        @SEC("dve")
                        def _(v, bp=bp, e=e, mp=mp, jj=jj, B0=B0):
                            wait(v, p2_sg[jj], p2_swf_free[jj], p2_cwB[jj])
                            v.tensor_tensor(s_swf[jj][:], pp[B0 + 2 * jj + 1][:], s_sg[jj][:], op=OP.mult)
                            ins = v.tensor_tensor(s_swf[jj][:], s_swf[jj][:], s_cwB[jj][:], op=OP.mult)
                            ee = sDVE.inc(ins)
                            pfree[B0 + 2 * jj] = pfree[B0 + 2 * jj + 1] = ee
                            p2_sg_free[jj] = ee
                            p2_swf[jj] = ee
                            if mp == 10:
                                p2_cwB_free[jj] = ee

                        @SEC("act")
                        def _(sc, bp=bp, e=e, mp=mp, jj=jj):
                            wait(sc, p2_swf[jj], ev_sw_free[0])
                            ins = sc.copy(s_sw[:, TB * (11 * jj + mp):TB * (11 * jj + mp + 1)],
                                          s_swf[jj][:])
                            ee = sACT.inc(ins)
                            p2_swf_free[jj] = ee
                            p2_swcast[(e, mp, jj)] = ee

                for m in range(16):
                    dbank = 2 * (m % 2)

                    @SEC("gp")
                    def _(gp, bp=bp, e=e, m=m):
                        parB = m % 2
                        wait(gp, ev_wpB_free[parB])
                        ins = gp.dma_start(out=s_wpB[parB][:, 0:11 * 128], in_=dn_h[e][m])
                        ev_wpB[parB] = sGP.inc(ins, 16)

                    @SEC("pe")
                    def _(te, bp=bp, e=e, m=m, dbank=dbank):
                        parB = m % 2
                        wait(te, ev_wpB[parB], pfree[dbank], pfree[dbank + 1],
                             *[p2_swcast[(e, mp, jj)] for mp in range(11) for jj in range(2)])
                        for k in range(11):
                            wslice = s_wpB[parB][:, 128 * k:128 * (k + 1)]
                            te.matmul(pp[dbank][:], wslice, s_sw[:, TB * k:TB * (k + 1)],
                                      start=(k == 0), stop=(k == 10))
                            ins = te.matmul(pp[dbank + 1][:], wslice,
                                            s_sw[:, TB * (11 + k):TB * (12 + k)],
                                            start=(k == 0), stop=(k == 10))
                        ee = sPE.inc(ins)
                        pready[dbank] = pready[dbank + 1] = ee
                        ev_wpB_free[parB] = ee
                        if m == 15:
                            ev_sw_free[0] = ee

                    @SEC("dve")
                    def _(v, bp=bp, e=e, m=m, dbank=dbank):
                        wait(v, pready[dbank])
                        if e == 0:
                            wait(v, ev_Y_free[0], ev_acc_free[0], ev_acc_free[1])
                            v.tensor_copy(s_acc[0][:, TB * m:TB * (m + 1)], pp[dbank][:])
                            ins = v.tensor_copy(s_acc[1][:, TB * m:TB * (m + 1)], pp[dbank + 1][:])
                        else:
                            v.tensor_tensor(s_acc[0][:, TB * m:TB * (m + 1)],
                                            s_acc[0][:, TB * m:TB * (m + 1)],
                                            pp[dbank][:], op=OP.add)
                            ins = v.tensor_tensor(s_acc[1][:, TB * m:TB * (m + 1)],
                                                  s_acc[1][:, TB * m:TB * (m + 1)],
                                                  pp[dbank + 1][:], op=OP.add)
                        ee = sDVE.inc(ins)
                        pfree[dbank] = pfree[dbank + 1] = ee
                        p2_accw[(bp, e, m)] = ee

        def p2_shared(b):
            j = b % 2
            bp = b // 2
            for mp in range(22):
                gbank = (mp % 2) * 2
                ubank = gbank + 1

                @SEC("gp")
                def _(gp, b=b, mp=mp):
                    parA = mp % 2
                    wait(gp, ev_wpA_free[parA])
                    sGP.inc(gp.dma_start(out=s_wpA[parA][:, 0:16 * 128], in_=sup_h[mp]), 16)
                    ins = gp.dma_start(out=s_wpA[parA][:, 16 * 128:32 * 128], in_=sup_h[mp + 22])
                    ev_wpA[parA] = sGP.inc(ins, 16)

                @SEC("pe")
                def _(te, b=b, mp=mp, j=j, gbank=gbank, ubank=ubank):
                    parA = mp % 2
                    wait(te, ev_wpA[parA], pfree[gbank], pfree[ubank])
                    for k in range(16):
                        te.matmul(pp[gbank][:], s_wpA[parA][:, 128 * k:128 * (k + 1)],
                                  s_xtbf[j][:, TB * k:TB * (k + 1)],
                                  start=(k == 0), stop=(k == 15))
                    for k in range(16):
                        ins = te.matmul(pp[ubank][:],
                                        s_wpA[parA][:, 16 * 128 + 128 * k:16 * 128 + 128 * (k + 1)],
                                        s_xtbf[j][:, TB * k:TB * (k + 1)],
                                        start=(k == 0), stop=(k == 15))
                    ee = sPE.inc(ins)
                    pready[gbank] = pready[ubank] = ee
                    ev_wpA_free[parA] = ee
                    if mp == 21:
                        ev_xtbf_free[j] = ee

                @SEC("act")
                def _(sc, b=b, mp=mp, gbank=gbank):
                    parA = mp % 2
                    wait(sc, pready[gbank], p2_sg_free[parA])
                    ins = sc.activation(s_sg[parA][:], pp[gbank][:], AF.Silu)
                    p2_sg[parA] = sACT.inc(ins)

                @SEC("dve")
                def _(v, b=b, mp=mp, gbank=gbank, ubank=ubank):
                    parA = mp % 2
                    wait(v, p2_sg[parA], p2_swf_free[parA])
                    ins = v.tensor_tensor(s_swf[parA][:], pp[ubank][:], s_sg[parA][:], op=OP.mult)
                    ee = sDVE.inc(ins)
                    pfree[gbank] = pfree[ubank] = ee
                    p2_sg_free[parA] = ee
                    p2_swf[parA] = ee

                @SEC("act")
                def _(sc, b=b, mp=mp):
                    parA = mp % 2
                    wait(sc, p2_swf[parA], ev_sw_free[0])
                    ins = sc.copy(s_sw[:, TB * mp:TB * (mp + 1)], s_swf[parA][:])
                    ee = sACT.inc(ins)
                    p2_swf_free[parA] = ee
                    p2_swcast[("sh", b, mp)] = ee

            for m in range(16):
                dbank = 4 + 2 * (m % 2)

                @SEC("gp")
                def _(gp, b=b, m=m):
                    parB = m % 2
                    wait(gp, ev_wpB_free[parB])
                    ins = gp.dma_start(out=s_wpB[parB][:, 0:22 * 128], in_=sdn_h[m])
                    ev_wpB[parB] = sGP.inc(ins, 16)

                @SEC("pe")
                def _(te, b=b, m=m, dbank=dbank):
                    parB = m % 2
                    wait(te, ev_wpB[parB], pfree[dbank],
                         *[p2_swcast[("sh", b, mp)] for mp in range(22)])
                    for k in range(22):
                        ins = te.matmul(pp[dbank][:], s_wpB[parB][:, 128 * k:128 * (k + 1)],
                                        s_sw[:, TB * k:TB * (k + 1)],
                                        start=(k == 0), stop=(k == 21))
                    ee = sPE.inc(ins)
                    pready[dbank] = ee
                    ev_wpB_free[parB] = ee
                    if m == 15:
                        ev_sw_free[0] = ee

                @SEC("dve")
                def _(v, b=b, m=m, dbank=dbank, j=j, bp=bp):
                    wait(v, pready[dbank], p2_accw[(bp, NE - 1, m)])
                    ins = v.tensor_tensor(s_acc[j][:, TB * m:TB * (m + 1)],
                                          s_acc[j][:, TB * m:TB * (m + 1)],
                                          pp[dbank][:], op=OP.add)
                    ee = sDVE.inc(ins)
                    pfree[dbank] = ee
                    p2_shw[(b, m)] = ee

            if "acc" in dbg:
                @SEC("sp")
                def _(sp, b=b, j=j):
                    wait(sp, *[p2_shw[(b, m)] for m in range(16)])
                    for m in range(16):
                        sSP.inc(sp.dma_start(out=dbg["acc"][b][128 * m:128 * (m + 1), :],
                                             in_=s_acc[j][:, TB * m:TB * (m + 1)]), 16)

        def p2_tail(b):
            j = b % 2
            for m in range(16):
                @SEC("sp")
                def _(sp, b=b, m=m, j=j):
                    wait(sp, p2_x2use.get((b, m - 2)), p2_xuse.get((b + 1, 14 + (m % 2))))
                    p2_x2ld[(b, m)] = sSP.inc(
                        sp.dma_start(out=s_x2[m % 2][:], in_=scr2[j].ap()[128 * m:128 * (m + 1), :]), 16)

                @SEC("dve")
                def _(v, b=b, m=m, j=j):
                    wait(v, p2_x2ld[(b, m)], p2_shw[(b, m)])
                    ins = v.tensor_tensor(s_acc[j][:, TB * m:TB * (m + 1)],
                                          s_acc[j][:, TB * m:TB * (m + 1)], s_x2[m % 2][:], op=OP.add)
                    ee = sDVE.inc(ins)
                    p2_x2use[(b, m)] = ee
                    p2_r2[(b, m)] = ee

                @SEC("act")
                def _(sc, b=b, m=m, j=j):
                    wait(sc, p2_r2[(b, m)], p2_sq_free[0], p2_sq2ws.get((b, m - 1)))
                    p2_sq2[(b, m)] = sACT.inc(
                        sc.square(s_sq[m % 2][:], s_acc[j][:, TB * m:TB * (m + 1)]))

                @SEC("pe")
                def _(te, b=b, m=m, j=j):
                    wait(te, p2_r2[(b, m)], p2_sq2[(b, m)])
                    if m == 0:
                        wait(te, pfree[4], pfree[5])
                    te.matmul(pp[4][0:1, :], s_on128[:], s_acc[j][:, TB * m:TB * (m + 1)],
                              start=(m == 0), stop=(m == 15))
                    ins = te.matmul(pp[5][0:1, :], s_on128[:], s_sq[m % 2][:],
                                    start=(m == 0), stop=(m == 15))
                    ee = sPE.inc(ins)
                    p2_sq_free[0] = ee
                    p2_sq2ws[(b, m)] = ee
                    if m == 15:
                        pready[4] = pready[5] = ee

            @SEC("act")
            def _(sc, b=b):
                wait(sc, pready[4])
                sc.mul(s_stat[:, 0 * TB:1 * TB], pp[4][0:1, :], 1.0 / H)
                ins = sc.mul(s_stat[:, 1 * TB:2 * TB], pp[5][0:1, :], 1.0 / H)
                ee = sACT.inc(ins)
                pfree[4] = pfree[5] = ee
                p2_st1[b] = ee

            @SEC("dve")
            def _(v, b=b):
                wait(v, p2_st1[b])
                v.tensor_tensor(s_stat[:, 2 * TB:3 * TB], s_stat[:, 0:TB],
                                s_stat[:, 0:TB], op=OP.mult)
                ins = v.tensor_tensor(s_stat[:, TB:2 * TB], s_stat[:, TB:2 * TB],
                                      s_stat[:, 2 * TB:3 * TB], op=OP.subtract)
                p2_st2[b] = sDVE.inc(ins)

            @SEC("act")
            def _(sc, b=b):
                wait(sc, p2_st2[b])
                sc.activation(s_stat[:, TB:2 * TB], s_stat[:, TB:2 * TB],
                              AF.Ln, bias=s_eps[0:1, :])
                ins = sc.activation(s_stat[:, TB:2 * TB], s_stat[:, TB:2 * TB],
                                    AF.Exp, scale=-0.5)
                p2_st3[b] = sACT.inc(ins)

            for t in range(4):
                bank = 6 + (t % 2)

                @SEC("pe")
                def _(te, b=b, t=t, bank=bank):
                    wait(te, p2_st3[b], pfree[bank])
                    te.transpose(pp[bank][0:128, 0:1],
                                 s_stat[0:1, 128 * t:128 * (t + 1)], s_id[0:1, 0:1])
                    ins = te.transpose(pp[bank][0:128, 1:2],
                                       s_stat[0:1, TB + 128 * t:TB + 128 * (t + 1)],
                                       s_id[0:1, 0:1])
                    pready[bank] = sPE.inc(ins)

                @SEC("act")
                def _(sc, b=b, t=t, bank=bank):
                    wait(sc, pready[bank])
                    ins = sc.copy(s_mrT[:, 2 * t:2 * t + 2], pp[bank][0:128, 0:2])
                    ee = sACT.inc(ins)
                    pfree[bank] = ee
                    p2_mrT[(b, t)] = ee

            for t in range(4):
                for m in range(16):
                    bank = 6 + (m % 2)

                    @SEC("pe")
                    def _(te, b=b, t=t, m=m, bank=bank, j=j):
                        wait(te, p2_r2[(b, m)], pfree[bank])
                        ins = te.transpose(pp[bank][0:128, 0:128],
                                           s_acc[j][:, TB * m + 128 * t:TB * m + 128 * (t + 1)],
                                           s_id[:])
                        pready[bank] = sPE.inc(ins)

                    @SEC("act")
                    def _(sc, b=b, t=t, m=m, bank=bank):
                        wait(sc, pready[bank], p2_row_free[0] if m == 0 else None)
                        ins = sc.copy(s_row[:, 128 * m:128 * (m + 1)], pp[bank][0:128, 0:128])
                        ee = sACT.inc(ins)
                        pfree[bank] = ee
                        p2_rowp[(b, t, m)] = ee

                @SEC("dve")
                def _(v, b=b, t=t):
                    wait(v, *[p2_rowp[(b, t, m)] for m in range(16)], p2_mrT[(b, t)])
                    ins = v.tensor_scalar(s_row[:], s_row[:], s_mrT[:, 2 * t:2 * t + 1],
                                          s_mrT[:, 2 * t + 1:2 * t + 2],
                                          op0=OP.subtract, op1=OP.mult)
                    p2_norm[(b, t)] = sDVE.inc(ins)

                @SEC("sp")
                def _(sp, b=b, t=t):
                    wait(sp, p2_norm[(b, t)])
                    ins = sp.dma_start(out=out[TB * b + 128 * t:TB * b + 128 * (t + 1), :],
                                       in_=s_row[:])
                    ee = sSP.inc(ins, 16)
                    p2_row_free[0] = ee
                    p2_out[(b, t)] = ee

            ev_acc_free[j] = (sPE, sPE.n)

        p2_xld = {}
        p2_xuse = {}
        p2_res = {}
        p2_sq = {}
        p2_stats = {}
        p2_resfree = {}
        ev_p2c2 = [None]
        p2_xnuse = {}
        p2_st1 = {}
        p2_st2 = {}
        p2_st3 = {}
        p2_st4 = {}
        p2_r2ld = {}
        p2_xn = {}
        p2_xtbf = {}
        p2_gsc = {}
        p2_xn_free = [None]
        p2_xn_free2 = [None]
        p2_sc16 = {}
        p2_sctok = {}
        p2_sm = {}
        p2_smx = {}
        p2_cw = {}
        p2_cwT = {}
        p2_cwB = [None, None]
        p2_cwB_free = [None, None]
        p2_sg = [None, None]
        p2_sg_free = [None, None]
        p2_swf = [None, None]
        p2_swf_free = [None, None]
        p2_swcast = {}
        p2_accw = {}
        p2_shw = {}
        p2_x2ld = {}
        p2_x2use = {}
        p2_r2 = {}
        p2_sq2 = {}
        p2_sq_free = [None]
        p2_sq2ws = {}
        p2_mrT = {}
        p2_rowp = {}
        p2_norm = {}
        p2_row_free = [None]
        p2_out = {}

        ev_p2c = [None]

        @SEC("sp")
        def _(sp):
            wait(sp, ev_ag_done[NSCH // 2 - 1])
            sSP.inc(sp.dma_start(out=s_gw[:], in_=gw_h), 16)
            ev_p2c[0] = sSP.inc(sp.dma_start(out=s_id[:], in_=id_dram.ap()), 16)

        @SEC("dve")
        def _(v):
            wait(v, ev_ag_done[NSCH // 2 - 1])
            ev_p2c2[0] = sDVE.inc(v.memset(s_on1[:], 1.0))

        for bp in range(NB // 2):
            p2_partA(2 * bp)
            p2_partA(2 * bp + 1)
            p2_experts_pair(bp)
            p2_shared(2 * bp)
            p2_shared(2 * bp + 1)
            p2_tail(2 * bp)
            p2_tail(2 * bp + 1)

        @SEC("sp")
        def _(sp):
            wait(sp, *[p2_out[(b, t)] for b in range(NB) for t in range(4)])
            sp.wait_ge(sSP.h, sSP.n)

        @SEC("gp")
        def _(gp):
            gp.wait_ge(sGP.h, sGP.n)
            gp.wait_ge(sCC.h, sCC.n)

    return nc


def _host_prepare(inputs):
    x = np.asarray(inputs["L0_attn_hidden_states"])[0]
    x_T = np.ascontiguousarray(x.T)
    x_host = np.ascontiguousarray(
        x_T.reshape(16, 128, NSCH, TS).transpose(2, 1, 0, 3).reshape(NSCH, 128, 16 * TS))
    wT = np.asarray(inputs["L0_attn_in_proj_w"]).T
    xw = np.ascontiguousarray(np.asarray(inputs["L0_attn_x_proj_w"]).T)
    dtw = np.ascontiguousarray(np.asarray(inputs["L0_attn_dt_proj_w"]).T)
    opT = np.asarray(inputs["L0_attn_out_proj_w"]).T
    op_h = np.ascontiguousarray(
        opT.reshape(32, 128, 16, 128).transpose(2, 1, 0, 3).reshape(16, 128, 32 * 128))
    gw = np.asarray(inputs["L0_ffn_gate_w"]).T
    gw_h = np.ascontiguousarray(
        gw.reshape(16, 128, NE).transpose(1, 0, 2).reshape(128, 16 * NE))
    upT = np.asarray(inputs["L0_ffn_expert_up_w"]).transpose(0, 2, 1)
    up_h = np.ascontiguousarray(
        upT.reshape(NE, 16, 128, 22, 128).transpose(0, 3, 2, 1, 4).reshape(NE, 22, 128, 16 * 128))
    dnT = np.asarray(inputs["L0_ffn_expert_down_w"]).transpose(0, 2, 1)
    dn_h = np.ascontiguousarray(
        dnT.reshape(NE, 11, 128, 16, 128).transpose(0, 3, 2, 1, 4).reshape(NE, 16, 128, 11 * 128))
    supT = np.asarray(inputs["L0_ffn_shared_up_w"]).T
    sup_h = np.ascontiguousarray(
        supT.reshape(16, 128, 44, 128).transpose(2, 1, 0, 3).reshape(44, 128, 16 * 128))
    sdnT = np.asarray(inputs["L0_ffn_shared_down_w"]).T
    sdn_h = np.ascontiguousarray(
        sdnT.reshape(22, 128, 16, 128).transpose(2, 1, 0, 3).reshape(16, 128, 22 * 128))

    def perchan(vec):
        return np.ascontiguousarray(vec.reshape(4, 128).T)

    convw_full = np.asarray(inputs["L0_attn_conv_w"])[:, 0, :]
    convb_full = np.asarray(inputs["L0_attn_conv_b"])
    dtb_full = np.asarray(inputs["L0_attn_dt_proj_b"])
    alog_full = np.asarray(inputs["L0_attn_A_log"])
    dvec_full = np.asarray(inputs["L0_attn_D"])

    in_maps = []
    for c in range(NCORES):
        cs = slice(DC * c, DC * (c + 1))
        win_c = np.concatenate([wT[:, DC * c:DC * (c + 1)],
                                wT[:, INTER + DC * c:INTER + DC * (c + 1)]], axis=1)
        win_h = np.ascontiguousarray(
            win_c.reshape(16, 128, 1024).transpose(1, 0, 2).reshape(128, 16 * 1024))
        xw_c = xw[cs]
        xw_h = np.ascontiguousarray(
            xw_c.reshape(4, 128, 160).transpose(1, 0, 2).reshape(128, 4 * 160))
        cwc = convw_full[cs]
        convw_h = np.ascontiguousarray(
            cwc.reshape(4, 128, 4).transpose(1, 0, 2).reshape(128, 16))
        alog_c = alog_full[cs]
        alog_h = np.ascontiguousarray(
            alog_c.reshape(4, 128, NST).transpose(1, 0, 2).reshape(128, 4 * NST))
        m = {
            "x_host": x_host,
            "x_my": np.ascontiguousarray(x_T[:, TOKC * c:TOKC * (c + 1)]),
            "win_h": win_h,
            "convw": convw_h,
            "convb": perchan(convb_full[cs]),
            "xw_h": xw_h,
            "dtwT": np.ascontiguousarray(dtw[:, cs]),
            "dtb": perchan(dtb_full[cs]),
            "alog": alog_h,
            "dvec": perchan(dvec_full[cs]),
            "op_h": op_h,
            "gw_h": gw_h,
            "up_h": up_h,
            "dn_h": dn_h,
            "sup_h": sup_h,
            "sdn_h": sdn_h,
        }
        in_maps.append({k: np.ascontiguousarray(v, dtype=np.float32) for k, v in m.items()})
    return in_maps


def _run(inputs):
    in_maps = _host_prepare(inputs)
    nc = build()
    res = run_bass_kernel_spmd(nc, in_maps, core_ids=list(range(NCORES)))
    return res


def kernel(**inputs):
    res = _run(inputs)
    parts = [res.results[c]["out"] for c in range(NCORES)]
    full = np.concatenate(parts, axis=0)
    return full.reshape(1, L, H).astype(np.float32)


# version 15 (final)
# speedup vs baseline: 1.1205x; 1.1205x over previous
import sys
if '/opt/trn_rl_repo' not in sys.path:
    sys.path.insert(0, '/opt/trn_rl_repo')
import contextlib
import numpy as np
import ml_dtypes
import concourse.bass as bass
import concourse.mybir as mybir
from concourse.alu_op_type import AluOpType as OP
from concourse.bass_utils import run_bass_kernel_spmd

F32 = mybir.dt.float32
BF16 = mybir.dt.bfloat16
AF = mybir.ActivationFunctionType
AX = mybir.AxisListType.X

NCORES = 8
L = 16384
H = 2048
INTER = 4096
DC = INTER // NCORES
NST = 16
RANK = 128
KC = 4
TS = 512
NSCH = L // TS
TOKC = L // NCORES
TB = 512
NB = TOKC // TB
NE = 16
DI = 1408
DSH = 2816
NEG = -1e30

DBG = set()


class Sem:
    def __init__(self, stack, nc, name):
        self.h = stack.enter_context(nc.semaphore(name))
        self.n = 0

    def inc(self, instr, k=1):
        instr.then_inc(self.h, k)
        self.n += k
        return (self, self.n)


def wait(eng, *evs):
    best = {}
    for e in evs:
        if e is None:
            continue
        s, n = e
        if n > best.get(id(s), (s, 0))[1]:
            best[id(s)] = (s, n)
    for s, n in best.values():
        eng.wait_ge(s.h, n)


def build():
    nc = bass.Bass("TRN2", target_bir_lowering=False, num_devices=NCORES)
    rg = [list(range(NCORES))]

    def din(name, shape, dt=F32):
        return nc.dram_tensor(name, shape, dt, kind="ExternalInput").ap()

    x_host = din("x_host", [NSCH, 128, 16 * TS])
    x_my = din("x_my", [H, TOKC])
    win_h = din("win_h", [128, 16 * 1024])
    convw = din("convw", [128, 4 * KC])
    convb = din("convb", [128, 4])
    xw_h = din("xw_h", [128, 4 * 160])
    dtwT = din("dtwT", [RANK, DC])
    dtb = din("dtb", [128, 4])
    alog = din("alog", [128, 4 * NST])
    dvec = din("dvec", [128, 4])
    op_h = din("op_h", [16, 128, 32 * 128])
    gw_h = din("gw_h", [128, NE * 16])
    up_h = din("up_h", [NE, 22, 128, 16 * 128])
    dn_h = din("dn_h", [NE, 16, 128, 11 * 128])
    sup_h = din("sup_h", [44, 128, 16 * 128])
    sdn_h = din("sdn_h", [16, 128, 22 * 128])

    out = nc.dram_tensor("out", [TOKC, H], F32, kind="ExternalOutput").ap()

    arb_i = [nc.dram_tensor(f"arb_i{p}", [160, TS], F32, kind="Internal") for p in range(2)]
    arb_o = [nc.dram_tensor(f"arb_o{p}", [160, TS], F32, kind="Internal") for p in range(2)]
    ag_i = [nc.dram_tensor(f"ag_i{p}", [DC, 2 * TS], BF16, kind="Internal") for p in range(2)]
    Y_all = nc.dram_tensor("Y_all", [NSCH // 2, INTER, 2 * TS], BF16, kind="Internal")
    scr1 = nc.dram_tensor("scr1", [H, TB], F32, kind="Internal")
    scr2 = [nc.dram_tensor(f"scr2_{j}", [H, TB], F32, kind="Internal") for j in range(2)]

    dbg = {}
    if DBG:
        if "u" in DBG:
            dbg["u"] = nc.dram_tensor("dbg_u", [DC, L], F32, kind="ExternalOutput").ap()
        if "gate" in DBG:
            dbg["gate"] = nc.dram_tensor("dbg_gate", [DC, L], F32, kind="ExternalOutput").ap()
        if "dt" in DBG:
            dbg["dt"] = nc.dram_tensor("dbg_dt", [DC, L], F32, kind="ExternalOutput").ap()
        if "ssm" in DBG:
            dbg["ssm"] = nc.dram_tensor("dbg_ssm", [160, L], F32, kind="ExternalOutput").ap()
        if "Y" in DBG:
            dbg["Y"] = nc.dram_tensor("dbg_Y", [NSCH // 2, INTER, 2 * TS], BF16, kind="ExternalOutput").ap()
        if "xt" in DBG:
            dbg["xt"] = nc.dram_tensor("dbg_xt", [NB, H, TB], F32, kind="ExternalOutput").ap()
        if "cw" in DBG:
            dbg["cw"] = nc.dram_tensor("dbg_cw", [NB, NE, TB], F32, kind="ExternalOutput").ap()
        if "g" in DBG:
            dbg["g"] = nc.dram_tensor("dbg_g", [NB, 128, 6 * NE], F32, kind="ExternalOutput").ap()
        if "acc" in DBG:
            dbg["acc"] = nc.dram_tensor("dbg_acc", [NB, H, TB], F32, kind="ExternalOutput").ap()

    ind_np = np.zeros((NST, NST * 128), dtype=np.float32)
    for n in range(NST):
        ind_np[n, 128 * n:128 * (n + 1)] = 1.0
    ind_dram = nc.inline_tensor(ind_np, name="ind_c")
    id_np = np.eye(128, dtype=np.float32)
    id_dram = nc.inline_tensor(id_np, name="id_c")

    es = contextlib.ExitStack()
    with es:
        base = (nc.sbuf_base + 31) & ~31
        cap = 207 * 1024
        es.enter_context(nc.sbuf_tensor("arena", [128, cap], mybir.dt.uint8))
        off = [base]

        def at(name, shape, dt):
            nbytes = int(np.prod(shape[1:])) * mybir.dt.size(dt)
            o = (off[0] + 31) & ~31
            t = nc.alloc_sbuf_tensor_at(name, shape, dt, offset=o)
            off[0] = o + nbytes
            return t

        s_xw = at("s_xw", [128, 4 * 160], BF16)
        s_dtw = at("s_dtw", [128, DC], BF16)
        s_A = at("s_A", [128, 4 * NST], F32)
        s_cw = at("s_cw", [128, 16], F32)
        s_cb = at("s_cb", [128, 4], F32)
        s_D = at("s_D", [128, 4], F32)
        s_dtb = at("s_dtb", [128, 4], F32)
        s_ind = at("s_ind", [NST, NST * 128], F32)
        s_on128 = at("s_on128", [128, 1], F32)
        s_sk = at("s_sk", [128, 4 * NST], F32)
        s_eps = at("s_eps", [128, 1], F32)
        s_neg = at("s_neg", [128, 1], F32)
        s_tiny = at("s_tiny", [128, 1], F32)
        ph_base = off[0]

        s_win = at("s_win", [128, 16 * 1024], BF16)
        s_x = [at(f"s_x{p}", [128, 16 * TS], BF16) for p in range(2)]
        s_upre = [at(f"s_upre{d}", [128, TS + 3], F32) for d in range(4)]
        s_u = [at(f"s_u{p}", [128, 4 * TS], F32) for p in range(2)]
        s_gate = [at(f"s_gate{p}", [128, 4 * TS], F32) for p in range(2)]
        s_dt = at("s_dt", [128, 4 * TS], F32)
        s_dtu = at("s_dtu", [128, 4 * TS], F32)
        s_ub = [at(f"s_ub{p}", [128, 4 * TS], BF16) for p in range(2)]
        s_sphi = at("s_sphi", [128, TS], F32)
        s_splo = at("s_splo", [32, TS], F32)
        s_dtr = at("s_dtr", [128, TS], F32)
        s_dtrb = at("s_dtrb", [128, TS], BF16)
        s_B = [at(f"s_B{p}", [16, TS], F32) for p in range(2)]
        s_C = [at(f"s_C{p}", [16, TS], F32) for p in range(2)]
        s_sp1 = at("s_sp1", [128, TS], F32)
        s_a = [at(f"s_a{p}", [128, 4 * TS], F32) for p in range(2)]
        s_b = at("s_b", [128, 4 * TS], F32)
        s_st = at("s_st", [128, 4 * TS], F32)
        s_yacc = at("s_yacc", [128, 4 * TS], F32)
        s_yp = at("s_yp", [128, TS], F32)
        p1_end = off[0]
        import os
        if os.environ.get('SBUFMAP'): print('SBUFMAP p1_end', p1_end, 'cap', 229376)
        assert p1_end - base < cap, f"phase1 SBUF overflow {p1_end - base}"

        off[0] = ph_base
        s_Y = at("s_Y", [128, 32 * TB], BF16)
        off[0] = ph_base
        s_acc = [at("s_acc0", [128, 16 * TB], F32)]
        s_acc.append(at("s_acc1", [128, 16 * TB], F32))
        s_xtbf = [at(f"s_xtbf{j}", [128, 16 * TB], BF16) for j in range(2)]
        s_wpA = [at(f"s_wpA{p}", [128, 32 * 128], BF16) for p in range(2)]
        s_wpB = [at(f"s_wpB{p}", [128, 22 * 128], BF16) for p in range(2)]
        s_sw = at("s_sw", [128, 22 * TB], BF16)
        s_res = [at(f"s_res{p}", [128, TB], F32) for p in range(2)]
        s_sq = [at(f"s_sq{p}", [128, TB], F32) for p in range(2)]
        s_x2 = [at(f"s_x2_{p}", [128, TB], F32) for p in range(2)]
        s_sg = [at(f"s_sg{p}", [128, TB], F32) for p in range(2)]
        s_swf = [at(f"s_swf{p}", [128, TB], F32) for p in range(2)]
        s_cwB = [at(f"s_cwB{j}", [128, TB], F32) for j in range(2)]
        s_xn = [at(f"s_xn{p}", [128, TB], F32) for p in range(2)]
        s_row = at("s_row", [128, H], F32)
        s_sc16 = at("s_sc16", [16, TB], F32)
        s_cwT = [at(f"s_cwT{j}", [16, TB], F32) for j in range(2)]
        s_cwtok = at("s_cwtok", [128, 4 * NE], F32)
        s_g1 = at("s_g1", [128, NE], F32)
        s_g2 = at("s_g2", [128, NE], F32)
        s_g2b = at("s_g2b", [128, NE], F32)
        s_gm = at("s_gm", [128, 4], F32)
        s_r1 = at("s_r1", [128, 1], F32)
        s_r2 = at("s_r2", [128, 1], F32)
        s_stat = at("s_stat", [1, 3 * TB], F32)
        s_mrT = at("s_mrT", [128, 8], F32)
        s_gw = at("s_gw", [128, NE * 16], F32)
        s_id = at("s_id", [128, 128], F32)
        s_on1 = at("s_on1", [1, 128], F32)
        s_gdbg = at("s_gdbg", [128, 6 * NE], F32) if DBG else None
        p2_end = off[0]
        if os.environ.get('SBUFMAP'): print('SBUFMAP p2_end', p2_end, 'base', base)
        assert p2_end - base < cap, f"phase2 SBUF overflow {p2_end - base}"

        pp = [es.enter_context(nc.psum_tensor(f"pp{i}", [128, 512], F32))
              for i in range(8)]
        pready = [None] * 8
        pfree = [None] * 8

        sPE = Sem(es, nc, "sPE")
        sACT = Sem(es, nc, "sACT")
        sDVE = Sem(es, nc, "sDVE")
        sSP = Sem(es, nc, "sSP")
        sGP = Sem(es, nc, "sGP")
        sCC = Sem(es, nc, "sCC")

        block = es.enter_context(nc.Block())

        def SEC(engname):
            return {"pe": block.tensor, "act": block.scalar, "dve": block.vector,
                    "sp": block.sync, "gp": block.gpsimd}[engname]

        ev_setup_gp = [None]
        ev_setup_sp = [None]
        ev_rank = [None]
        rank_reg = [None]

        @SEC("gp")
        def _(gp):
            e = None
            e = sGP.inc(gp.dma_start(out=s_win[:], in_=win_h), 16)
            e = sGP.inc(gp.dma_start(out=s_xw[:], in_=xw_h), 16)
            e = sGP.inc(gp.dma_start(out=s_dtw[:], in_=dtwT), 16)
            ev_setup_gp[0] = e

        @SEC("sp")
        def _(sp):
            e = None
            e = sSP.inc(sp.dma_start(out=s_A[:], in_=alog), 16)
            e = sSP.inc(sp.dma_start(out=s_cw[:], in_=convw), 16)
            e = sSP.inc(sp.dma_start(out=s_cb[:], in_=convb), 16)
            e = sSP.inc(sp.dma_start(out=s_D[:], in_=dvec), 16)
            e = sSP.inc(sp.dma_start(out=s_dtb[:], in_=dtb), 16)
            e = sSP.inc(sp.dma_start(out=s_ind[:], in_=ind_dram.ap()), 16)
            ev_setup_sp[0] = e
            rk = sp.alloc_register()
            sp.cc_rank_ld(rk, replica_groups=rg)
            rank_reg[0] = rk
            ev_rank[0] = (sSP, sSP.n)

        ev_A = [None]
        ev_init = [None]

        @SEC("act")
        def _(sc):
            wait(sc, ev_setup_sp[0])
            sc.activation(s_A[:], s_A[:], AF.Exp)
            ev_A[0] = sACT.inc(sc.mul(s_A[:], s_A[:], -1.0))

        @SEC("dve")
        def _(v):
            v.memset(s_sk[:], 0.0)
            for d in range(4):
                v.memset(s_upre[d][:, 0:3], 0.0)
            v.memset(s_on128[:], 1.0)
            v.memset(s_eps[:], 1e-5)
            v.memset(s_neg[:], NEG)
            v.memset(s_tiny[:], 1e-20)
            ev_init[0] = sDVE.inc(v.memset(s_sk[:, 0:1], 0.0))

        ev_x = [None, None]
        ev_x_free = [None, None]
        ev_upre_free = [None]
        ev_ug = [None, None]
        ev_ug_free = [None, None]
        ev_sphl = [None]
        ev_ar_out = [None, None]
        ev_ar_in_free = [None, None]
        ev_dtrb = [None]
        ev_dt = [None]
        ev_a = [None, None]
        ev_a_free = [None, None]
        ev_bc = [None, None]
        ev_bc_free = [None, None]
        ev_y = [None]
        ev_ag_in = [None, None]
        ev_ag_done = [None] * NSCH
        ev_sk_dma = [None]

        def stageA(s):
            par = s % 2

            @SEC("gp")
            def _(gp, s=s, par=par):
                wait(gp, ev_x_free[par], ev_setup_gp[0])
                ev_x[par] = sGP.inc(gp.dma_start(out=s_x[par][:], in_=x_host[s]), 16)

            for m in range(8):
                bank = m % 2

                @SEC("pe")
                def _(te, s=s, par=par, m=m, bank=bank):
                    wait(te, pfree[bank], ev_x[par], ev_A[0], ev_init[0])
                    for k in range(16):
                        ins = te.matmul(pp[bank][:], s_win[:, 1024 * k + 128 * m:1024 * k + 128 * (m + 1)],
                                        s_x[par][:, TS * k:TS * (k + 1)],
                                        start=(k == 0), stop=(k == 15))
                    e = sPE.inc(ins)
                    pready[bank] = e
                    if m == 7:
                        ev_x_free[par] = e

                @SEC("act")
                def _(sc, s=s, par=par, m=m, bank=bank):
                    wait(sc, pready[bank])
                    if m < 4:
                        wait(sc, ev_upre_free[0])
                        ins = sc.copy(s_upre[m][:, 3:3 + TS], pp[bank][:])
                    else:
                        wait(sc, ev_ug_free[par])
                        ins = sc.activation(s_gate[par][:, TS * (m - 4):TS * (m - 3)],
                                            pp[bank][:], AF.Silu)
                    pfree[bank] = sACT.inc(ins)

            @SEC("dve")
            def _(v, s=s, par=par):
                wait(v, pfree[0], pfree[1])
                for d in range(4):
                    v.tensor_scalar(s_u[par][:, TS * d:TS * (d + 1)], s_upre[d][:, 0:TS],
                                    s_cw[:, 4 * d:4 * d + 1], s_cb[:, d:d + 1],
                                    op0=OP.mult, op1=OP.add)
                    for j in range(1, 4):
                        v.scalar_tensor_tensor(s_u[par][:, TS * d:TS * (d + 1)],
                                               s_upre[d][:, j:j + TS],
                                               s_cw[:, 4 * d + j:4 * d + j + 1],
                                               s_u[par][:, TS * d:TS * (d + 1)],
                                               op0=OP.mult, op1=OP.add)
                ins = None
                for d in range(4):
                    ins = v.tensor_copy(s_upre[d][:, 0:3], s_upre[d][:, TS:TS + 3])
                ev_upre_free[0] = sDVE.inc(ins)

            @SEC("act")
            def _(sc, s=s, par=par):
                wait(sc, ev_upre_free[0], ev_ub_free[par])
                sc.activation(s_u[par][:], s_u[par][:], AF.Silu)
                ev_ug[par] = sACT.inc(sc.copy(s_ub[par][:], s_u[par][:]))

            @SEC("pe")
            def _(te, s=s, par=par):
                wait(te, ev_ug[par], pfree[2], pfree[3])
                for k in range(4):
                    i1 = te.matmul(pp[2][:], s_xw[:, 160 * k:160 * k + 128],
                                   s_ub[par][:, TS * k:TS * (k + 1)],
                                   start=(k == 0), stop=(k == 3))
                for k in range(4):
                    i2 = te.matmul(pp[3][0:32, :], s_xw[:, 160 * k + 128:160 * (k + 1)],
                                   s_ub[par][:, TS * k:TS * (k + 1)],
                                   start=(k == 0), stop=(k == 3))
                pready[2] = pready[3] = sPE.inc(i2)
                ev_ub_free[par] = pready[2]

            @SEC("act")
            def _(sc, s=s, par=par):
                wait(sc, pready[2], stA_ar.get(s - 1))
                sc.copy(s_sphi[:], pp[2][:])
                ins = sc.copy(s_splo[:], pp[3][0:32, :])
                pfree[2] = pfree[3] = ev_sphl[0] = sACT.inc(ins)

            @SEC("gp")
            def _(gp, s=s, par=par):
                wait(gp, ev_sphl[0], ev_ar_in_free[par])
                sGP.inc(gp.dma_start(out=arb_i[par].ap()[0:128, :], in_=s_sphi[:]), 16)
                ev = sGP.inc(gp.dma_start(out=arb_i[par].ap()[128:160, :], in_=s_splo[:]), 16)
                stA_ar[s] = ev

            @SEC("gp")
            def _(gp, s=s, par=par):
                wait(gp, stA_ar[s], ev_ar_out[par])
                ins = gp.collective_compute(
                    "AllReduce", OP.add, replica_groups=rg,
                    ins=[arb_i[par].ap()], outs=[arb_o[par].ap()])
                e = sCC.inc(ins)
                ev_ar_in_free[par] = e
                stB_ar[s] = e

        stA_ar = {}
        stB_ar = {}

        def stageB(s):
            par = s % 2

            @SEC("sp")
            def _(sp, s=s, par=par):
                wait(sp, stB_ar[s], ev_dtrb[0], stB_bclast.get(s - 2))
                sSP.inc(sp.dma_start(out=s_dtr[:], in_=arb_o[par].ap()[0:128, :]), 16)
                sSP.inc(sp.dma_start(out=s_B[par][:], in_=arb_o[par].ap()[128:144, :]), 16)
                ev = sSP.inc(sp.dma_start(out=s_C[par][:], in_=arb_o[par].ap()[144:160, :]), 16)
                ev_ar_out[par] = ev
                stB_ld[s] = ev
                if "ssm" in dbg:
                    sSP.inc(sp.dma_start(out=dbg["ssm"][:, s * TS:(s + 1) * TS],
                                         in_=arb_o[par].ap()), 16)

            @SEC("act")
            def _(sc, s=s):
                wait(sc, stB_ld[s], stB_dtpe.get(s - 1))
                ev_dtrb[0] = sACT.inc(sc.copy(s_dtrb[:], s_dtr[:]))

            for m in range(4):
                bank = 4 + (m % 2)

                @SEC("pe")
                def _(te, s=s, m=m, bank=bank):
                    wait(te, pfree[bank], ev_dtrb[0])
                    ins = te.matmul(pp[bank][:], s_dtw[:, 128 * m:128 * (m + 1)],
                                    s_dtrb[:], start=True, stop=True)
                    pready[bank] = sPE.inc(ins)
                    if m == 3:
                        stB_dtpe[s] = pready[bank]

                @SEC("act")
                def _(sc, s=s, m=m, bank=bank):
                    wait(sc, pready[bank], ev_dt[0], stB_alast.get(s - 1))
                    sc.activation(s_sp1[:], pp[bank][:], AF.Exp, bias=s_dtb[:, m:m + 1])
                    ins = sc.activation(s_dt[:, TS * m:TS * (m + 1)], s_sp1[:], AF.Ln,
                                        bias=1.0)
                    e = sACT.inc(ins)
                    pfree[bank] = e
                    stB_dt[(s, m)] = e

            @SEC("dve")
            def _(v, s=s, par=par):
                wait(v, ev_ug[par], *[stB_dt[(s, m)] for m in range(4)])
                ins = None
                for d in range(4):
                    ins = v.tensor_tensor(s_dtu[:, TS * d:TS * (d + 1)],
                                          s_dt[:, TS * d:TS * (d + 1)],
                                          s_u[par][:, TS * d:TS * (d + 1)], op=OP.mult)
                ev_dt[0] = sDVE.inc(ins)
                if "dt" in dbg or "u" in dbg or "gate" in dbg:
                    stB_dbg[s] = ev_dt[0]

            if dbg:
                @SEC("sp")
                def _(sp, s=s, par=par):
                    if "dt" in dbg or "u" in dbg or "gate" in dbg:
                        wait(sp, stB_dbg.get(s))
                        for d in range(4):
                            sl = slice(s * TS, (s + 1) * TS)
                            if "u" in dbg:
                                sSP.inc(sp.dma_start(out=dbg["u"][128 * d:128 * (d + 1), sl],
                                                     in_=s_u[par][:, TS * d:TS * (d + 1)]), 16)
                            if "gate" in dbg:
                                sSP.inc(sp.dma_start(out=dbg["gate"][128 * d:128 * (d + 1), sl],
                                                     in_=s_gate[par][:, TS * d:TS * (d + 1)]), 16)
                            if "dt" in dbg:
                                sSP.inc(sp.dma_start(out=dbg["dt"][128 * d:128 * (d + 1), sl],
                                                     in_=s_dt[:, TS * d:TS * (d + 1)]), 16)

            for n in range(NST):
                npar = n % 2
                bbank = 6 if npar == 0 else 4
                cbank = 7 if npar == 0 else 5

                @SEC("pe")
                def _(te, s=s, n=n, npar=npar):
                    wait(te, stB_ld[s], pfree[bbank], pfree[cbank])
                    te.matmul(pp[bbank][:], s_ind[:, 128 * n:128 * (n + 1)], s_B[par][:],
                              start=True, stop=True)
                    ins = te.matmul(pp[cbank][:], s_ind[:, 128 * n:128 * (n + 1)], s_C[par][:],
                                    start=True, stop=True)
                    pready[bbank] = pready[cbank] = sPE.inc(ins)
                    if n == NST - 1:
                        stB_bclast[s] = pready[bbank]

                @SEC("act")
                def _(sc, s=s, n=n, npar=npar):
                    wait(sc, ev_a_free[npar], ev_dt[0])
                    ins = None
                    for d in range(4):
                        ins = sc.activation(s_a[npar][:, TS * d:TS * (d + 1)],
                                            s_dt[:, TS * d:TS * (d + 1)], AF.Exp,
                                            scale=s_A[:, NST * d + n:NST * d + n + 1])
                    e = sACT.inc(ins)
                    ev_a[npar] = e
                    if n == NST - 1:
                        stB_alast[s] = e

                @SEC("dve")
                def _(v, s=s, n=n, npar=npar, bbank=bbank, cbank=cbank):
                    wait(v, ev_a[npar], pready[bbank])
                    if n == 0:
                        wait(v, ev_ag_sbuf_free[0])
                    bc4 = pp[bbank][:].unsqueeze(1).broadcast_to([128, 4, TS])
                    v.tensor_tensor(s_b[:].rearrange("p (d t) -> p d t", d=4),
                                    s_dtu[:].rearrange("p (d t) -> p d t", d=4),
                                    bc4, op=OP.mult)
                    for d in range(4):
                        dsl = slice(TS * d, TS * (d + 1))
                        v.tensor_tensor_scan(s_st[:, dsl], s_a[npar][:, dsl], s_b[:, dsl],
                                             s_sk[:, NST * d + n:NST * d + n + 1],
                                             op0=OP.mult, op1=OP.add)
                    cc4 = pp[cbank][:].unsqueeze(1).broadcast_to([128, 4, TS])
                    if n == 0:
                        ins = v.tensor_tensor(s_yacc[:].rearrange("p (d t) -> p d t", d=4),
                                              s_st[:].rearrange("p (d t) -> p d t", d=4),
                                              cc4, op=OP.mult)
                    else:
                        v.tensor_tensor(s_b[:].rearrange("p (d t) -> p d t", d=4),
                                        s_st[:].rearrange("p (d t) -> p d t", d=4),
                                        cc4, op=OP.mult)
                        ins = v.tensor_tensor(s_yacc[:], s_yacc[:], s_b[:], op=OP.add)
                    ins = v.tensor_copy(
                        s_sk[:].rearrange("p (d n2) -> p d n2", d=4)[:, :, n:n + 1],
                        s_st[:].rearrange("p (d t) -> p d t", d=4)[:, :, TS - 1:TS])
                    e = sDVE.inc(ins)
                    pfree[bbank] = pfree[cbank] = e
                    ev_a_free[npar] = e

            @SEC("dve")
            def _(v, s=s, par=par):
                wait(v, ev_ag_sbuf_free[0])
                ins = None
                for d in range(4):
                    dsl = slice(TS * d, TS * (d + 1))
                    v.scalar_tensor_tensor(s_yp[:], s_u[par][:, dsl], s_D[:, d:d + 1],
                                           s_yacc[:, dsl], op0=OP.mult, op1=OP.add)
                    ins = v.tensor_tensor(s_yacc[:, dsl], s_yp[:], s_gate[par][:, dsl],
                                          op=OP.mult)
                e = sDVE.inc(ins)
                ev_ug_free[par] = e
                stB_y[s] = e

            @SEC("gp")
            def _(gp, s=s, par=par):
                q = s // 2
                half = s % 2
                wait(gp, stB_y[s])
                if half == 0:
                    wait(gp, ev_ag_i_free[q % 2])
                e = None
                for d in range(4):
                    e = sGP.inc(gp.dma_start(
                        out=ag_i[q % 2].ap()[128 * d:128 * (d + 1), TS * half:TS * (half + 1)],
                        in_=s_yacc[:, TS * d:TS * (d + 1)]), 16)
                ev_ag_in[half] = e
                ev_ag_sbuf_free[0] = e

            if s % 2 == 1:
                @SEC("gp")
                def _(gp, s=s):
                    q = s // 2
                    wait(gp, ev_ag_in[0], ev_ag_in[1])
                    ins = gp.collective_compute(
                        "AllGather", OP.bypass, replica_groups=rg,
                        ins=[ag_i[q % 2].ap()], outs=[Y_all.ap()[q]])
                    e = sCC.inc(ins)
                    ev_ag_done[q] = e
                    ev_ag_i_free[q % 2] = e

        stB_ld = {}
        stB_dt = {}
        stB_alast = {}
        stB_bclast = {}
        stB_dtpe = {}
        stB_dbg = {}
        stB_y = {}
        ev_ag_sbuf_free = [None]
        ev_ag_i_free = [None, None]
        ev_ub_free = [None, None]

        for s in range(NSCH + 1):
            if s < NSCH:
                stageA(s)
            if s >= 1:
                stageB(s - 1)

        if "Y" in dbg:
            @SEC("gp")
            def _(gp):
                wait(gp, *[ev_ag_done[q] for q in range(NSCH // 2)])
                sGP.inc(gp.dma_start(out=dbg["Y"], in_=Y_all.ap()), 16)

        CHUNK_ELEMS = INTER * TS

        ev_Y = [None]
        ev_Y_free = [None]
        ev_acc_free = [None, None]
        ev_xtbf_free = [None, None]
        ev_wpA = [None, None]
        ev_wpA_free = [None, None]
        ev_wpB = [None, None]
        ev_wpB_free = [None, None]
        ev_sw_free = [None]

        def p2_partA(b):
            j = b % 2

            @SEC("sp")
            def _(sp, b=b):
                wait(sp, ev_rank[0], ev_acc_free[0], ev_ag_done[NSCH // 2 - 1], ev_Y_free[0])
                rk = rank_reg[0]
                with sp.register(f"yoff{b}") as yo, sp.register(f"yh{b}") as yh:
                    sp.reg_mul(yo, rk, 4)
                    sp.reg_add(yo, yo, b)
                    sp.reg_alu(yh, yo, 1, OP.bitwise_and)
                    sp.reg_mul(yh, yh, TS)
                    sp.reg_alu(yo, yo, 1, OP.arith_shift_right)
                    sp.reg_mul(yo, yo, INTER * 2 * TS)
                    sp.reg_alu(yo, yo, yh, OP.add)
                    ysrc = bass.AP(Y_all, yo, [[2 * TS, INTER], [1, TS]])
                    ysrc = ysrc.rearrange("(k p) t -> p k t", p=128)
                    ev_Y[0] = sSP.inc(sp.dma_start(out=s_Y[:].rearrange("p (k t) -> p k t", k=32), in_=ysrc), 16)

            for m in range(16):
                bank = m % 2

                @SEC("gp")
                def _(gp, b=b, m=m):
                    par = m % 2
                    wait(gp, ev_wpA_free[par])
                    if b == 0 and m == 0:
                        wait(gp, ev_ag_done[NSCH // 2 - 1])
                    ev_wpA[par] = sGP.inc(gp.dma_start(out=s_wpA[par][:], in_=op_h[m]), 16)

                @SEC("pe")
                def _(te, b=b, m=m, bank=bank):
                    par = m % 2
                    wait(te, pfree[bank], ev_Y[0], ev_wpA[par])
                    if b == 0 and m == 0:
                        wait(te, ev_p2c[0], ev_p2c2[0])
                    for k in range(32):
                        ins = te.matmul(pp[bank][:], s_wpA[par][:, 128 * k:128 * (k + 1)],
                                        s_Y[:, TB * k:TB * (k + 1)],
                                        start=(k == 0), stop=(k == 31))
                    e = sPE.inc(ins)
                    pready[bank] = e
                    ev_wpA_free[par] = e
                    if m == 15:
                        ev_Y_free[0] = e

                @SEC("sp")
                def _(sp, b=b, m=m):
                    wait(sp, p2_xuse.get((b, m - 2)), p2_x2use.get((b - 1, 14 + (m % 2))))
                    p2_xld[(b, m)] = sSP.inc(
                        sp.dma_start(out=s_x2[m % 2][:], in_=x_my[128 * m:128 * (m + 1),
                                                                  TB * b:TB * (b + 1)]), 16)

                @SEC("dve")
                def _(v, b=b, m=m, bank=bank):
                    wait(v, pready[bank], p2_xld[(b, m)], p2_resfree.get((b, m - 2)),
                         p2_xnuse.get((b - 1, 14 + (m % 2))))
                    ins = v.tensor_tensor(s_res[m % 2][:], pp[bank][:], s_x2[m % 2][:], op=OP.add)
                    e = sDVE.inc(ins)
                    pfree[bank] = e
                    p2_res[(b, m)] = e
                    p2_xuse[(b, m)] = e

                @SEC("act")
                def _(sc, b=b, m=m):
                    wait(sc, p2_res[(b, m)], p2_stats.get((b, m - 1)))
                    p2_sq[(b, m)] = sACT.inc(sc.square(s_sq[m % 2][:], s_res[m % 2][:]))

                def emit_stats(m2, b=b):
                    @SEC("pe")
                    def _(te, b=b, m2=m2):
                        wait(te, p2_res[(b, m2)], p2_sq[(b, m2)])
                        if m2 == 0:
                            wait(te, pfree[4], pfree[5])
                        te.matmul(pp[4][0:1, :], s_on128[:], s_res[m2 % 2][:],
                                  start=(m2 == 0), stop=(m2 == 15))
                        ins = te.matmul(pp[5][0:1, :], s_on128[:], s_sq[m2 % 2][:],
                                        start=(m2 == 0), stop=(m2 == 15))
                        e = sPE.inc(ins)
                        p2_stats[(b, m2)] = e
                        if m2 == 15:
                            pready[4] = pready[5] = e

                    @SEC("sp")
                    def _(sp, b=b, m2=m2):
                        wait(sp, p2_res[(b, m2)], p2_stats[(b, m2)])
                        ins = sp.dma_start(out=scr1.ap()[128 * m2:128 * (m2 + 1), :],
                                           in_=s_res[m2 % 2][:])
                        e = sSP.inc(ins, 16)
                        p2_resfree[(b, m2)] = e
                if m >= 1:
                    emit_stats(m - 1)
                if m == 15:
                    emit_stats(15)

            @SEC("act")
            def _(sc, b=b):
                wait(sc, pready[4])
                sc.mul(s_stat[:, 0 * TB:1 * TB], pp[4][0:1, :], 1.0 / H)
                ins = sc.mul(s_stat[:, 1 * TB:2 * TB], pp[5][0:1, :], 1.0 / H)
                e = sACT.inc(ins)
                pfree[4] = pfree[5] = e
                p2_st1[b] = e

            @SEC("dve")
            def _(v, b=b):
                wait(v, p2_st1[b])
                v.tensor_tensor(s_stat[:, 2 * TB:3 * TB], s_stat[:, 0:TB],
                                s_stat[:, 0:TB], op=OP.mult)
                ins = v.tensor_tensor(s_stat[:, TB:2 * TB], s_stat[:, TB:2 * TB],
                                      s_stat[:, 2 * TB:3 * TB], op=OP.subtract)
                p2_st2[b] = sDVE.inc(ins)

            @SEC("act")
            def _(sc, b=b):
                wait(sc, p2_st2[b])
                sc.activation(s_stat[:, TB:2 * TB], s_stat[:, TB:2 * TB],
                              AF.Ln, bias=s_eps[0:1, :])
                ins = sc.activation(s_stat[:, TB:2 * TB], s_stat[:, TB:2 * TB],
                                    AF.Exp, scale=-0.5)
                p2_st3[b] = sACT.inc(ins)

            @SEC("dve")
            def _(v, b=b):
                wait(v, p2_st3[b])
                ins = v.tensor_tensor(s_stat[:, 2 * TB:3 * TB], s_stat[:, 0:TB],
                                      s_stat[:, TB:2 * TB], op=OP.mult)
                sDVE.inc(ins)
                v.wait_ge(sDVE.h, sDVE.n)
                ins = v.tensor_scalar(s_stat[:, 2 * TB:3 * TB], s_stat[:, 2 * TB:3 * TB],
                                      -1.0, 1.0, op0=OP.mult, op1=OP.mult)
                p2_st4[b] = sDVE.inc(ins)

            @SEC("pe")
            def _(te, b=b):
                wait(te, p2_st4[b], pfree[6], pfree[7])
                te.matmul(pp[6][:], s_on1[:], s_stat[0:1, TB:2 * TB],
                          start=True, stop=True)
                ins = te.matmul(pp[7][:], s_on1[:], s_stat[0:1, 2 * TB:3 * TB],
                                start=True, stop=True)
                e = sPE.inc(ins)
                pready[6] = pready[7] = e

            for m in range(16):
                @SEC("sp")
                def _(sp, b=b, m=m):
                    wait(sp, p2_xnuse.get((b, m - 2)), p2_resfree.get((b, 14 + (m % 2))))
                    p2_r2ld[(b, m)] = sSP.inc(
                        sp.dma_start(out=s_res[m % 2][:], in_=scr1.ap()[128 * m:128 * (m + 1), :]), 16)

                @SEC("dve")
                def _(v, b=b, m=m):
                    wait(v, p2_r2ld[(b, m)], pready[6], p2_xnfree.get((b, m - 2)))
                    v.tensor_tensor(s_xn[m % 2][:], s_res[m % 2][:], pp[6][:], op=OP.mult)
                    ins = v.tensor_tensor(s_xn[m % 2][:], s_xn[m % 2][:], pp[7][:], op=OP.add)
                    p2_xn[(b, m)] = sDVE.inc(ins)
                    p2_xnuse[(b, m)] = p2_xn[(b, m)]

                def emit_gate(m2, b=b, j=j):
                    @SEC("pe")
                    def _(te, b=b, m2=m2):
                        wait(te, p2_xn[(b, m2)])
                        if m2 == 0:
                            wait(te, pfree[5])
                        ins = te.matmul(pp[5][0:16, :], s_gw[:, 16 * m2:16 * (m2 + 1)],
                                        s_xn[m2 % 2][:], start=(m2 == 0), stop=(m2 == 15))
                        e = sPE.inc(ins)
                        p2_gsc[(b, m2)] = e
                        if m2 == 15:
                            pready[5] = e

                    @SEC("act")
                    def _(sc, b=b, m2=m2, j=j):
                        wait(sc, p2_xn[(b, m2)], ev_xtbf_free[j])
                        ins = sc.copy(s_xtbf[j][:, TB * m2:TB * (m2 + 1)], s_xn[m2 % 2][:])
                        p2_xtbf[(b, m2)] = sACT.inc(ins)

                    @SEC("sp")
                    def _(sp, b=b, m2=m2, j=j):
                        wait(sp, p2_xn[(b, m2)], p2_gsc[(b, m2)], p2_xtbf[(b, m2)])
                        ins = sp.dma_start(out=scr2[j].ap()[128 * m2:128 * (m2 + 1), :],
                                           in_=s_xn[m2 % 2][:])
                        e = sSP.inc(ins, 16)
                        p2_xn_free[0] = e
                        p2_xnfree[(b, m2)] = e
                        if m2 == 15:
                            pfree[6] = pfree[7] = e
                if m >= 1:
                    emit_gate(m - 1)
                if m == 15:
                    emit_gate(15)

            if "xt" in dbg:
                @SEC("sp")
                def _(sp, b=b, j=j):
                    wait(sp, p2_xn_free[0])
                    sSP.inc(sp.dma_start(out=dbg["xt"][b], in_=scr2[j].ap()), 16)

            @SEC("act")
            def _(sc, b=b):
                wait(sc, pready[5])
                ins = sc.copy(s_sc16[:], pp[5][0:16, :])
                e = sACT.inc(ins)
                pfree[5] = e
                p2_sc16[b] = e

            for t in range(4):
                bank = 6 + (t % 2)

                @SEC("pe")
                def _(te, b=b, t=t, bank=bank):
                    wait(te, p2_sc16[b], pfree[bank])
                    ins = te.transpose(pp[bank][0:128, 0:16], s_sc16[:, 128 * t:128 * (t + 1)],
                                       s_id[0:16, 0:16])
                    pready[bank] = sPE.inc(ins)

                @SEC("act")
                def _(sc, b=b, t=t, bank=bank):
                    wait(sc, pready[bank])
                    ins = sc.copy(s_cwtok[:, NE * t:NE * (t + 1)], pp[bank][0:128, 0:16])
                    pfree[bank] = sACT.inc(ins)
                    p2_sctok[(b, t)] = (sACT, sACT.n)

            for t in range(4):
                @SEC("dve")
                def _(v, b=b, t=t):
                    wait(v, p2_sctok[(b, t)])
                    cwt = s_cwtok[:, NE * t:NE * (t + 1)]

                    def F(ins):
                        sDVE.inc(ins)
                        v.wait_ge(sDVE.h, sDVE.n)
                    F(v.tensor_reduce(s_r1[:], cwt, axis=AX, op=OP.max))
                    F(v.tensor_scalar(s_r1[:], s_r1[:], -1.0, 1.0, op0=OP.mult, op1=OP.mult))
                    p2_sm[(b, t)] = sDVE.inc(v.tensor_copy(s_g1[:], cwt))

                @SEC("act")
                def _(sc, b=b, t=t):
                    wait(sc, p2_sm[(b, t)])
                    cwt = s_cwtok[:, NE * t:NE * (t + 1)]
                    ins = sc.activation(cwt, s_g1[:], AF.Exp, bias=s_r1[:, 0:1])
                    p2_smx[(b, t)] = sACT.inc(ins)

                @SEC("dve")
                def _(v, b=b, t=t):
                    wait(v, p2_smx[(b, t)])
                    cwt = s_cwtok[:, NE * t:NE * (t + 1)]

                    def F(ins):
                        sDVE.inc(ins)
                        v.wait_ge(sDVE.h, sDVE.n)
                    F(v.tensor_reduce(s_r1[:], cwt, axis=AX, op=OP.add))
                    F(v.reciprocal(s_r2[:], s_r1[:]))
                    F(v.tensor_scalar(cwt, cwt, s_r2[:, 0:1], 1.0, op0=OP.mult, op1=OP.mult))
                    gv = cwt.rearrange("p (g e) -> p g e", g=4)
                    F(v.tensor_reduce(s_gm[:], gv, axis=AX, op=OP.max))
                    F(v.tensor_reduce(s_r1[:], s_gm[:], axis=AX, op=OP.max))
                    F(v.tensor_scalar(s_g2[:, 0:4], s_gm[:], s_r1[:, 0:1], 1.0,
                                      op0=OP.is_ge, op1=OP.mult))
                    F(v.scalar_tensor_tensor(s_g2[:, 0:4], s_g2[:, 0:4], s_neg[:, 0:1], s_gm[:],
                                             op0=OP.mult, op1=OP.add))
                    F(v.tensor_reduce(s_r1[:], s_g2[:, 0:4], axis=AX, op=OP.max))
                    F(v.tensor_scalar(s_gm[:], s_gm[:], s_r1[:, 0:1], 1.0,
                                      op0=OP.is_ge, op1=OP.mult))
                    gmb = s_gm[:].unsqueeze(2).broadcast_to([128, 4, 4])
                    F(v.tensor_tensor(s_g1[:].rearrange("p (g e) -> p g e", g=4), gv, gmb,
                                      op=OP.mult))
                    F(v.tensor_copy(s_g2[:], s_g1[:]))
                    for _i in range(3):
                        F(v.tensor_reduce(s_r1[:], s_g2[:], axis=AX, op=OP.max))
                        F(v.tensor_scalar(s_g2b[:], s_g2[:], s_r1[:, 0:1], 1.0,
                                          op0=OP.is_ge, op1=OP.mult))
                        F(v.scalar_tensor_tensor(s_g2[:], s_g2b[:], s_neg[:, 0:1], s_g2[:],
                                                 op0=OP.mult, op1=OP.add))
                    F(v.tensor_reduce(s_r1[:], s_g2[:], axis=AX, op=OP.max))
                    F(v.tensor_scalar(s_g2b[:], s_g1[:], s_r1[:, 0:1], 1.0,
                                      op0=OP.is_ge, op1=OP.mult))
                    F(v.tensor_tensor(s_g1[:], s_g1[:], s_g2b[:], op=OP.mult))
                    F(v.tensor_reduce(s_r1[:], s_g1[:], axis=AX, op=OP.add))
                    F(v.tensor_scalar(s_r1[:], s_r1[:], s_tiny[:, 0:1], 1.0, op0=OP.add, op1=OP.mult))
                    F(v.reciprocal(s_r2[:], s_r1[:]))
                    ins = v.tensor_scalar(cwt, s_g1[:], s_r2[:, 0:1], 1.0,
                                          op0=OP.mult, op1=OP.mult)
                    p2_cw[(b, t)] = sDVE.inc(ins)

            for t in range(4):
                bank = 6 + (t % 2)

                @SEC("pe")
                def _(te, b=b, t=t, bank=bank):
                    wait(te, p2_cw[(b, t)], pfree[bank])
                    ins = te.transpose(pp[bank][0:16, 0:128],
                                       s_cwtok[:, NE * t:NE * (t + 1)], s_id[:])
                    pready[bank] = sPE.inc(ins)

                @SEC("act")
                def _(sc, b=b, t=t, bank=bank, j=j):
                    wait(sc, pready[bank])
                    ins = sc.copy(s_cwT[j][:, 128 * t:128 * (t + 1)], pp[bank][0:16, 0:128])
                    e = sACT.inc(ins)
                    pfree[bank] = e
                    p2_cwT[(b, t)] = e

            if "cw" in dbg:
                @SEC("sp")
                def _(sp, b=b, j=j):
                    wait(sp, *[p2_cwT[(b, t)] for t in range(4)])
                    sSP.inc(sp.dma_start(out=dbg["cw"][b], in_=s_cwT[j][:]), 16)

        def p2_experts_pair(bp):
            b0, b1 = 2 * bp, 2 * bp + 1
            for e in range(NE):
                @SEC("pe")
                def _(te, bp=bp, e=e):
                    wait(te, *[p2_cwT[(b0, t)] for t in range(4)],
                         *[p2_cwT[(b1, t)] for t in range(4)], pfree[6], pfree[7])
                    te.matmul(pp[6][:], s_ind[:, 128 * e:128 * (e + 1)], s_cwT[0][:],
                              start=True, stop=True)
                    ins = te.matmul(pp[7][:], s_ind[:, 128 * e:128 * (e + 1)], s_cwT[1][:],
                                    start=True, stop=True)
                    pready[6] = pready[7] = sPE.inc(ins)

                @SEC("act")
                def _(sc, bp=bp, e=e):
                    wait(sc, pready[6], p2_cwB_free[0], p2_cwB_free[1])
                    sc.copy(s_cwB[0][:], pp[6][:])
                    ins = sc.copy(s_cwB[1][:], pp[7][:])
                    e2 = sACT.inc(ins)
                    pfree[6] = pfree[7] = e2
                    p2_cwB[0] = p2_cwB[1] = e2

                for mp in range(11):
                    B0 = (mp % 2) * 4

                    @SEC("gp")
                    def _(gp, bp=bp, e=e, mp=mp):
                        parA = mp % 2
                        wait(gp, ev_wpA_free[parA])
                        sGP.inc(gp.dma_start(out=s_wpA[parA][:, 0:16 * 128], in_=up_h[e][mp]), 16)
                        ins = gp.dma_start(out=s_wpA[parA][:, 16 * 128:32 * 128], in_=up_h[e][mp + 11])
                        ev_wpA[parA] = sGP.inc(ins, 16)

                    @SEC("pe")
                    def _(te, bp=bp, e=e, mp=mp, B0=B0):
                        parA = mp % 2
                        wait(te, ev_wpA[parA], pfree[B0], pfree[B0 + 1], pfree[B0 + 2],
                             pfree[B0 + 3])
                        if mp == 0 and e == 0:
                            wait(te, *[p2_xtbf[(b0, m)] for m in range(16)],
                                 *[p2_xtbf[(b1, m)] for m in range(16)])
                        for k in range(16):
                            wslice = s_wpA[parA][:, 128 * k:128 * (k + 1)]
                            te.matmul(pp[B0][:], wslice, s_xtbf[0][:, TB * k:TB * (k + 1)],
                                      start=(k == 0), stop=(k == 15))
                            te.matmul(pp[B0 + 2][:], wslice, s_xtbf[1][:, TB * k:TB * (k + 1)],
                                      start=(k == 0), stop=(k == 15))
                        for k in range(16):
                            wslice = s_wpA[parA][:, 16 * 128 + 128 * k:16 * 128 + 128 * (k + 1)]
                            te.matmul(pp[B0 + 1][:], wslice, s_xtbf[0][:, TB * k:TB * (k + 1)],
                                      start=(k == 0), stop=(k == 15))
                            ins = te.matmul(pp[B0 + 3][:], wslice, s_xtbf[1][:, TB * k:TB * (k + 1)],
                                            start=(k == 0), stop=(k == 15))
                        ee = sPE.inc(ins)
                        for i in range(4):
                            pready[B0 + i] = ee
                        ev_wpA_free[parA] = ee

                    for jj in range(2):
                        @SEC("act")
                        def _(sc, bp=bp, e=e, mp=mp, jj=jj, B0=B0):
                            wait(sc, pready[B0 + 2 * jj], p2_sg_free[jj])
                            ins = sc.activation(s_sg[jj][:], pp[B0 + 2 * jj][:], AF.Silu)
                            p2_sg[jj] = sACT.inc(ins)

                        @SEC("dve")
                        def _(v, bp=bp, e=e, mp=mp, jj=jj, B0=B0):
                            wait(v, p2_sg[jj], p2_swf_free[jj], p2_cwB[jj])
                            v.tensor_tensor(s_swf[jj][:], pp[B0 + 2 * jj + 1][:], s_sg[jj][:], op=OP.mult)
                            ins = v.tensor_tensor(s_swf[jj][:], s_swf[jj][:], s_cwB[jj][:], op=OP.mult)
                            ee = sDVE.inc(ins)
                            pfree[B0 + 2 * jj] = pfree[B0 + 2 * jj + 1] = ee
                            p2_sg_free[jj] = ee
                            p2_swf[jj] = ee
                            if mp == 10:
                                p2_cwB_free[jj] = ee

                        @SEC("act")
                        def _(sc, bp=bp, e=e, mp=mp, jj=jj):
                            wait(sc, p2_swf[jj], ev_sw_free[0])
                            ins = sc.copy(s_sw[:, TB * (11 * jj + mp):TB * (11 * jj + mp + 1)],
                                          s_swf[jj][:])
                            ee = sACT.inc(ins)
                            p2_swf_free[jj] = ee
                            p2_swcast[(e, mp, jj)] = ee

                for m in range(16):
                    dbank = 2 * (m % 2)

                    @SEC("gp")
                    def _(gp, bp=bp, e=e, m=m):
                        parB = m % 2
                        wait(gp, ev_wpB_free[parB])
                        ins = gp.dma_start(out=s_wpB[parB][:, 0:11 * 128], in_=dn_h[e][m])
                        ev_wpB[parB] = sGP.inc(ins, 16)

                    @SEC("pe")
                    def _(te, bp=bp, e=e, m=m, dbank=dbank):
                        parB = m % 2
                        wait(te, ev_wpB[parB], pfree[dbank], pfree[dbank + 1],
                             *[p2_swcast[(e, mp, jj)] for mp in range(11) for jj in range(2)])
                        for k in range(11):
                            wslice = s_wpB[parB][:, 128 * k:128 * (k + 1)]
                            te.matmul(pp[dbank][:], wslice, s_sw[:, TB * k:TB * (k + 1)],
                                      start=(k == 0), stop=(k == 10))
                            ins = te.matmul(pp[dbank + 1][:], wslice,
                                            s_sw[:, TB * (11 + k):TB * (12 + k)],
                                            start=(k == 0), stop=(k == 10))
                        ee = sPE.inc(ins)
                        pready[dbank] = pready[dbank + 1] = ee
                        ev_wpB_free[parB] = ee
                        if m == 15:
                            ev_sw_free[0] = ee

                    @SEC("dve")
                    def _(v, bp=bp, e=e, m=m, dbank=dbank):
                        wait(v, pready[dbank])
                        if e == 0:
                            wait(v, ev_Y_free[0], ev_acc_free[0], ev_acc_free[1])
                            v.tensor_copy(s_acc[0][:, TB * m:TB * (m + 1)], pp[dbank][:])
                            ins = v.tensor_copy(s_acc[1][:, TB * m:TB * (m + 1)], pp[dbank + 1][:])
                        else:
                            v.tensor_tensor(s_acc[0][:, TB * m:TB * (m + 1)],
                                            s_acc[0][:, TB * m:TB * (m + 1)],
                                            pp[dbank][:], op=OP.add)
                            ins = v.tensor_tensor(s_acc[1][:, TB * m:TB * (m + 1)],
                                                  s_acc[1][:, TB * m:TB * (m + 1)],
                                                  pp[dbank + 1][:], op=OP.add)
                        ee = sDVE.inc(ins)
                        pfree[dbank] = pfree[dbank + 1] = ee
                        p2_accw[(bp, e, m)] = ee

        def p2_shared(b):
            j = b % 2
            bp = b // 2
            for mp in range(22):
                gbank = (mp % 2) * 2
                ubank = gbank + 1

                @SEC("gp")
                def _(gp, b=b, mp=mp):
                    parA = mp % 2
                    wait(gp, ev_wpA_free[parA])
                    sGP.inc(gp.dma_start(out=s_wpA[parA][:, 0:16 * 128], in_=sup_h[mp]), 16)
                    ins = gp.dma_start(out=s_wpA[parA][:, 16 * 128:32 * 128], in_=sup_h[mp + 22])
                    ev_wpA[parA] = sGP.inc(ins, 16)

                @SEC("pe")
                def _(te, b=b, mp=mp, j=j, gbank=gbank, ubank=ubank):
                    parA = mp % 2
                    wait(te, ev_wpA[parA], pfree[gbank], pfree[ubank])
                    for k in range(16):
                        te.matmul(pp[gbank][:], s_wpA[parA][:, 128 * k:128 * (k + 1)],
                                  s_xtbf[j][:, TB * k:TB * (k + 1)],
                                  start=(k == 0), stop=(k == 15))
                    for k in range(16):
                        ins = te.matmul(pp[ubank][:],
                                        s_wpA[parA][:, 16 * 128 + 128 * k:16 * 128 + 128 * (k + 1)],
                                        s_xtbf[j][:, TB * k:TB * (k + 1)],
                                        start=(k == 0), stop=(k == 15))
                    ee = sPE.inc(ins)
                    pready[gbank] = pready[ubank] = ee
                    ev_wpA_free[parA] = ee
                    if mp == 21:
                        ev_xtbf_free[j] = ee

                @SEC("act")
                def _(sc, b=b, mp=mp, gbank=gbank):
                    parA = mp % 2
                    wait(sc, pready[gbank], p2_sg_free[parA])
                    ins = sc.activation(s_sg[parA][:], pp[gbank][:], AF.Silu)
                    p2_sg[parA] = sACT.inc(ins)

                @SEC("dve")
                def _(v, b=b, mp=mp, gbank=gbank, ubank=ubank):
                    parA = mp % 2
                    wait(v, p2_sg[parA], p2_swf_free[parA])
                    ins = v.tensor_tensor(s_swf[parA][:], pp[ubank][:], s_sg[parA][:], op=OP.mult)
                    ee = sDVE.inc(ins)
                    pfree[gbank] = pfree[ubank] = ee
                    p2_sg_free[parA] = ee
                    p2_swf[parA] = ee

                @SEC("act")
                def _(sc, b=b, mp=mp):
                    parA = mp % 2
                    wait(sc, p2_swf[parA], ev_sw_free[0])
                    ins = sc.copy(s_sw[:, TB * mp:TB * (mp + 1)], s_swf[parA][:])
                    ee = sACT.inc(ins)
                    p2_swf_free[parA] = ee
                    p2_swcast[("sh", b, mp)] = ee

            for m in range(16):
                dbank = 4 + 2 * (m % 2)

                @SEC("gp")
                def _(gp, b=b, m=m):
                    parB = m % 2
                    wait(gp, ev_wpB_free[parB])
                    ins = gp.dma_start(out=s_wpB[parB][:, 0:22 * 128], in_=sdn_h[m])
                    ev_wpB[parB] = sGP.inc(ins, 16)

                @SEC("pe")
                def _(te, b=b, m=m, dbank=dbank):
                    parB = m % 2
                    wait(te, ev_wpB[parB], pfree[dbank],
                         *[p2_swcast[("sh", b, mp)] for mp in range(22)])
                    for k in range(22):
                        ins = te.matmul(pp[dbank][:], s_wpB[parB][:, 128 * k:128 * (k + 1)],
                                        s_sw[:, TB * k:TB * (k + 1)],
                                        start=(k == 0), stop=(k == 21))
                    ee = sPE.inc(ins)
                    pready[dbank] = ee
                    ev_wpB_free[parB] = ee
                    if m == 15:
                        ev_sw_free[0] = ee

                @SEC("dve")
                def _(v, b=b, m=m, dbank=dbank, j=j, bp=bp):
                    wait(v, pready[dbank], p2_accw[(bp, NE - 1, m)])
                    ins = v.tensor_tensor(s_acc[j][:, TB * m:TB * (m + 1)],
                                          s_acc[j][:, TB * m:TB * (m + 1)],
                                          pp[dbank][:], op=OP.add)
                    ee = sDVE.inc(ins)
                    pfree[dbank] = ee
                    p2_shw[(b, m)] = ee

            if "acc" in dbg:
                @SEC("sp")
                def _(sp, b=b, j=j):
                    wait(sp, *[p2_shw[(b, m)] for m in range(16)])
                    for m in range(16):
                        sSP.inc(sp.dma_start(out=dbg["acc"][b][128 * m:128 * (m + 1), :],
                                             in_=s_acc[j][:, TB * m:TB * (m + 1)]), 16)

        def p2_tail(b):
            j = b % 2
            for m in range(16):
                @SEC("sp")
                def _(sp, b=b, m=m, j=j):
                    wait(sp, p2_x2use.get((b, m - 2)), p2_xuse.get((b + 1, 14 + (m % 2))))
                    p2_x2ld[(b, m)] = sSP.inc(
                        sp.dma_start(out=s_x2[m % 2][:], in_=scr2[j].ap()[128 * m:128 * (m + 1), :]), 16)

                @SEC("dve")
                def _(v, b=b, m=m, j=j):
                    wait(v, p2_x2ld[(b, m)], p2_shw[(b, m)])
                    ins = v.tensor_tensor(s_acc[j][:, TB * m:TB * (m + 1)],
                                          s_acc[j][:, TB * m:TB * (m + 1)], s_x2[m % 2][:], op=OP.add)
                    ee = sDVE.inc(ins)
                    p2_x2use[(b, m)] = ee
                    p2_r2[(b, m)] = ee

                @SEC("act")
                def _(sc, b=b, m=m, j=j):
                    wait(sc, p2_r2[(b, m)], p2_sq_free[0], p2_sq2ws.get((b, m - 1)))
                    p2_sq2[(b, m)] = sACT.inc(
                        sc.square(s_sq[m % 2][:], s_acc[j][:, TB * m:TB * (m + 1)]))

                @SEC("pe")
                def _(te, b=b, m=m, j=j):
                    wait(te, p2_r2[(b, m)], p2_sq2[(b, m)])
                    if m == 0:
                        wait(te, pfree[4], pfree[5])
                    te.matmul(pp[4][0:1, :], s_on128[:], s_acc[j][:, TB * m:TB * (m + 1)],
                              start=(m == 0), stop=(m == 15))
                    ins = te.matmul(pp[5][0:1, :], s_on128[:], s_sq[m % 2][:],
                                    start=(m == 0), stop=(m == 15))
                    ee = sPE.inc(ins)
                    p2_sq_free[0] = ee
                    p2_sq2ws[(b, m)] = ee
                    if m == 15:
                        pready[4] = pready[5] = ee

            @SEC("act")
            def _(sc, b=b):
                wait(sc, pready[4])
                sc.mul(s_stat[:, 0 * TB:1 * TB], pp[4][0:1, :], 1.0 / H)
                ins = sc.mul(s_stat[:, 1 * TB:2 * TB], pp[5][0:1, :], 1.0 / H)
                ee = sACT.inc(ins)
                pfree[4] = pfree[5] = ee
                p2_st1[b] = ee

            @SEC("dve")
            def _(v, b=b):
                wait(v, p2_st1[b])
                v.tensor_tensor(s_stat[:, 2 * TB:3 * TB], s_stat[:, 0:TB],
                                s_stat[:, 0:TB], op=OP.mult)
                ins = v.tensor_tensor(s_stat[:, TB:2 * TB], s_stat[:, TB:2 * TB],
                                      s_stat[:, 2 * TB:3 * TB], op=OP.subtract)
                p2_st2[b] = sDVE.inc(ins)

            @SEC("act")
            def _(sc, b=b):
                wait(sc, p2_st2[b])
                sc.activation(s_stat[:, TB:2 * TB], s_stat[:, TB:2 * TB],
                              AF.Ln, bias=s_eps[0:1, :])
                ins = sc.activation(s_stat[:, TB:2 * TB], s_stat[:, TB:2 * TB],
                                    AF.Exp, scale=-0.5)
                p2_st3[b] = sACT.inc(ins)

            for t in range(4):
                bank = 6 + (t % 2)

                @SEC("pe")
                def _(te, b=b, t=t, bank=bank):
                    wait(te, p2_st3[b], pfree[bank])
                    te.transpose(pp[bank][0:128, 0:1],
                                 s_stat[0:1, 128 * t:128 * (t + 1)], s_id[0:1, 0:1])
                    ins = te.transpose(pp[bank][0:128, 1:2],
                                       s_stat[0:1, TB + 128 * t:TB + 128 * (t + 1)],
                                       s_id[0:1, 0:1])
                    pready[bank] = sPE.inc(ins)

                @SEC("act")
                def _(sc, b=b, t=t, bank=bank):
                    wait(sc, pready[bank])
                    ins = sc.copy(s_mrT[:, 2 * t:2 * t + 2], pp[bank][0:128, 0:2])
                    ee = sACT.inc(ins)
                    pfree[bank] = ee
                    p2_mrT[(b, t)] = ee

            for t in range(4):
                for m in range(16):
                    bank = 6 + (m % 2)

                    @SEC("pe")
                    def _(te, b=b, t=t, m=m, bank=bank, j=j):
                        wait(te, p2_r2[(b, m)], pfree[bank])
                        ins = te.transpose(pp[bank][0:128, 0:128],
                                           s_acc[j][:, TB * m + 128 * t:TB * m + 128 * (t + 1)],
                                           s_id[:])
                        pready[bank] = sPE.inc(ins)

                    @SEC("act")
                    def _(sc, b=b, t=t, m=m, bank=bank):
                        wait(sc, pready[bank], p2_row_free[0] if m == 0 else None)
                        ins = sc.copy(s_row[:, 128 * m:128 * (m + 1)], pp[bank][0:128, 0:128])
                        ee = sACT.inc(ins)
                        pfree[bank] = ee
                        p2_rowp[(b, t, m)] = ee

                @SEC("dve")
                def _(v, b=b, t=t):
                    wait(v, *[p2_rowp[(b, t, m)] for m in range(16)], p2_mrT[(b, t)])
                    ins = v.tensor_scalar(s_row[:], s_row[:], s_mrT[:, 2 * t:2 * t + 1],
                                          s_mrT[:, 2 * t + 1:2 * t + 2],
                                          op0=OP.subtract, op1=OP.mult)
                    p2_norm[(b, t)] = sDVE.inc(ins)

                @SEC("sp")
                def _(sp, b=b, t=t):
                    wait(sp, p2_norm[(b, t)])
                    ins = sp.dma_start(out=out[TB * b + 128 * t:TB * b + 128 * (t + 1), :],
                                       in_=s_row[:])
                    ee = sSP.inc(ins, 16)
                    p2_row_free[0] = ee
                    p2_out[(b, t)] = ee

            ev_acc_free[j] = (sPE, sPE.n)

        p2_xld = {}
        p2_xuse = {}
        p2_res = {}
        p2_sq = {}
        p2_stats = {}
        p2_resfree = {}
        ev_p2c2 = [None]
        p2_xnuse = {}
        p2_st1 = {}
        p2_st2 = {}
        p2_st3 = {}
        p2_st4 = {}
        p2_r2ld = {}
        p2_xn = {}
        p2_xtbf = {}
        p2_gsc = {}
        p2_xn_free = [None]
        p2_xnfree = {}
        p2_sc16 = {}
        p2_sctok = {}
        p2_sm = {}
        p2_smx = {}
        p2_cw = {}
        p2_cwT = {}
        p2_cwB = [None, None]
        p2_cwB_free = [None, None]
        p2_sg = [None, None]
        p2_sg_free = [None, None]
        p2_swf = [None, None]
        p2_swf_free = [None, None]
        p2_swcast = {}
        p2_accw = {}
        p2_shw = {}
        p2_x2ld = {}
        p2_x2use = {}
        p2_r2 = {}
        p2_sq2 = {}
        p2_sq_free = [None]
        p2_sq2ws = {}
        p2_mrT = {}
        p2_rowp = {}
        p2_norm = {}
        p2_row_free = [None]
        p2_out = {}

        ev_p2c = [None]

        @SEC("sp")
        def _(sp):
            wait(sp, ev_ag_done[NSCH // 2 - 1])
            sSP.inc(sp.dma_start(out=s_gw[:], in_=gw_h), 16)
            ev_p2c[0] = sSP.inc(sp.dma_start(out=s_id[:], in_=id_dram.ap()), 16)

        @SEC("dve")
        def _(v):
            wait(v, ev_ag_done[NSCH // 2 - 1])
            ev_p2c2[0] = sDVE.inc(v.memset(s_on1[:], 1.0))

        for bp in range(NB // 2):
            p2_partA(2 * bp)
            p2_partA(2 * bp + 1)
            p2_experts_pair(bp)
            p2_shared(2 * bp)
            p2_shared(2 * bp + 1)
            p2_tail(2 * bp)
            p2_tail(2 * bp + 1)

        @SEC("sp")
        def _(sp):
            wait(sp, *[p2_out[(b, t)] for b in range(NB) for t in range(4)])
            sp.wait_ge(sSP.h, sSP.n)

        @SEC("gp")
        def _(gp):
            gp.wait_ge(sGP.h, sGP.n)
            gp.wait_ge(sCC.h, sCC.n)

    return nc


def _host_prepare(inputs):
    x = np.asarray(inputs["L0_attn_hidden_states"])[0]
    x_T = np.ascontiguousarray(x.T)
    x_host = np.ascontiguousarray(
        x_T.reshape(16, 128, NSCH, TS).transpose(2, 1, 0, 3).reshape(NSCH, 128, 16 * TS))
    wT = np.asarray(inputs["L0_attn_in_proj_w"]).T
    xw = np.ascontiguousarray(np.asarray(inputs["L0_attn_x_proj_w"]).T)
    dtw = np.ascontiguousarray(np.asarray(inputs["L0_attn_dt_proj_w"]).T)
    opT = np.asarray(inputs["L0_attn_out_proj_w"]).T
    op_h = np.ascontiguousarray(
        opT.reshape(32, 128, 16, 128).transpose(2, 1, 0, 3).reshape(16, 128, 32 * 128))
    gw = np.asarray(inputs["L0_ffn_gate_w"]).T
    gw_h = np.ascontiguousarray(
        gw.reshape(16, 128, NE).transpose(1, 0, 2).reshape(128, 16 * NE))
    upT = np.asarray(inputs["L0_ffn_expert_up_w"]).transpose(0, 2, 1)
    up_h = np.ascontiguousarray(
        upT.reshape(NE, 16, 128, 22, 128).transpose(0, 3, 2, 1, 4).reshape(NE, 22, 128, 16 * 128))
    dnT = np.asarray(inputs["L0_ffn_expert_down_w"]).transpose(0, 2, 1)
    dn_h = np.ascontiguousarray(
        dnT.reshape(NE, 11, 128, 16, 128).transpose(0, 3, 2, 1, 4).reshape(NE, 16, 128, 11 * 128))
    supT = np.asarray(inputs["L0_ffn_shared_up_w"]).T
    sup_h = np.ascontiguousarray(
        supT.reshape(16, 128, 44, 128).transpose(2, 1, 0, 3).reshape(44, 128, 16 * 128))
    sdnT = np.asarray(inputs["L0_ffn_shared_down_w"]).T
    sdn_h = np.ascontiguousarray(
        sdnT.reshape(22, 128, 16, 128).transpose(2, 1, 0, 3).reshape(16, 128, 22 * 128))

    def perchan(vec):
        return np.ascontiguousarray(vec.reshape(4, 128).T)

    convw_full = np.asarray(inputs["L0_attn_conv_w"])[:, 0, :]
    convb_full = np.asarray(inputs["L0_attn_conv_b"])
    dtb_full = np.asarray(inputs["L0_attn_dt_proj_b"])
    alog_full = np.asarray(inputs["L0_attn_A_log"])
    dvec_full = np.asarray(inputs["L0_attn_D"])

    in_maps = []
    for c in range(NCORES):
        cs = slice(DC * c, DC * (c + 1))
        win_c = np.concatenate([wT[:, DC * c:DC * (c + 1)],
                                wT[:, INTER + DC * c:INTER + DC * (c + 1)]], axis=1)
        win_h = np.ascontiguousarray(
            win_c.reshape(16, 128, 1024).transpose(1, 0, 2).reshape(128, 16 * 1024))
        xw_c = xw[cs]
        xw_h = np.ascontiguousarray(
            xw_c.reshape(4, 128, 160).transpose(1, 0, 2).reshape(128, 4 * 160))
        cwc = convw_full[cs]
        convw_h = np.ascontiguousarray(
            cwc.reshape(4, 128, 4).transpose(1, 0, 2).reshape(128, 16))
        alog_c = alog_full[cs]
        alog_h = np.ascontiguousarray(
            alog_c.reshape(4, 128, NST).transpose(1, 0, 2).reshape(128, 4 * NST))
        m = {
            "x_host": x_host,
            "x_my": np.ascontiguousarray(x_T[:, TOKC * c:TOKC * (c + 1)]),
            "win_h": win_h,
            "convw": convw_h,
            "convb": perchan(convb_full[cs]),
            "xw_h": xw_h,
            "dtwT": np.ascontiguousarray(dtw[:, cs]),
            "dtb": perchan(dtb_full[cs]),
            "alog": alog_h,
            "dvec": perchan(dvec_full[cs]),
            "op_h": op_h,
            "gw_h": gw_h,
            "up_h": up_h,
            "dn_h": dn_h,
            "sup_h": sup_h,
            "sdn_h": sdn_h,
        }
        in_maps.append({k: np.ascontiguousarray(v, dtype=np.float32) for k, v in m.items()})
    return in_maps


def _run(inputs):
    in_maps = _host_prepare(inputs)
    nc = build()
    res = run_bass_kernel_spmd(nc, in_maps, core_ids=list(range(NCORES)))
    return res


def kernel(**inputs):
    res = _run(inputs)
    parts = [res.results[c]["out"] for c in range(NCORES)]
    full = np.concatenate(parts, axis=0)
    return full.reshape(1, L, H).astype(np.float32)
